# version 01
# baseline (speedup 1.0000x reference)
import contextlib
import numpy as np
import ml_dtypes
import concourse.bass as bass
import concourse.mybir as mybir
from concourse.bass_utils import run_bass_kernel_spmd

F32 = mybir.dt.float32
BF16 = mybir.dt.bfloat16
I32 = mybir.dt.int32
AF = mybir.ActivationFunctionType
ALU = mybir.AluOpType
AX = mybir.AxisListType

MODE = "multi"

NCORE = 8
B_LOC = 2
S = 2048
D = 1024
T_LOC = B_LOC * S
NT = T_LOC // 128
NSLAB = T_LOC // 512
EPS_LN = 1e-5
EPS_RMS = 1e-6
NEGBIG = -1.0e30
BF = ml_dtypes.bfloat16

ENGINES = ("sync", "gpsimd", "tensor", "vector", "scalar")


class Prog:
    def __init__(self):
        self.recs = {e: [] for e in ENGINES}
        self.last_w = {}
        self.readers = {}

    def emit(self, eng, fn, reads=(), writes=(), inc=1):
        deps = []
        for b in reads:
            lw = self.last_w.get(b)
            if lw is not None:
                deps.append(lw)
        for b in writes:
            lw = self.last_w.get(b)
            if lw is not None:
                deps.append(lw)
            deps.extend(self.readers.get(b, ()))
        dd = {}
        for (de, di) in deps:
            if de == eng == "tensor":
                continue
            if dd.get(de, -1) < di:
                dd[de] = di
        idx = len(self.recs[eng])
        self.recs[eng].append({"fn": fn, "deps": dd, "inc": inc, "car": False})
        me = (eng, idx)
        for b in reads:
            self.readers.setdefault(b, []).append(me)
        for b in writes:
            self.last_w[b] = me
            self.readers[b] = []
        return me

    def barrier(self):
        lasts = {e: len(self.recs[e]) - 1 for e in ENGINES if self.recs[e]}
        for e in ENGINES:
            dd = {de: di for de, di in lasts.items() if de != e}
            self.recs[e].append(
                {"fn": lambda eng: eng.nop(), "deps": dd, "inc": 1,
                 "car": False})
        self.last_w = {}
        self.readers = {}

    R_DMA = {"sync": 16, "gpsimd": 8, "tensor": 2, "vector": 2, "scalar": 2}

    def replay(self, nc):
        for e in ENGINES:
            for r in self.recs[e]:
                for de, di in r["deps"].items():
                    self.recs[de][di]["car"] = True
        for e in ENGINES:
            for r in self.recs[e]:
                if r["inc"] == 16:
                    r["car"] = True
        cumc = {}
        cntd = {}
        dmak = {}
        for e in ENGINES:
            c = 0
            n = 0
            cs = []
            ds = []
            ks = []
            for r in self.recs[e]:
                kk = -1
                if r["car"]:
                    if r["inc"] == 16:
                        kk = n
                        n += 1
                    else:
                        c += 1
                cs.append(c)
                ds.append(n)
                ks.append(kk)
            cumc[e] = cs
            cntd[e] = ds
            dmak[e] = ks

        def dma_waits(e, n):
            R = self.R_DMA[e]
            out = []
            for r_ in range(min(R, n)):
                c_r = n // R + (1 if r_ < n % R else 0)
                out.append((r_, 16 * c_r))
            return out

        with contextlib.ExitStack() as stk:
            semc = {e: stk.enter_context(nc.semaphore(name="semc_" + e))
                    for e in ENGINES}
            semd = {e: [stk.enter_context(
                nc.semaphore(name="semd_%s_%d" % (e, i)))
                for i in range(self.R_DMA[e])] for e in ENGINES}
            block = stk.enter_context(nc.Block())

            def mk(e):
                def body(eng):
                    wc = {o: 0 for o in ENGINES}
                    wd = {o: {} for o in ENGINES}
                    R = self.R_DMA[e]
                    for r in self.recs[e]:
                        for de in sorted(r["deps"]):
                            di = r["deps"][de]
                            tc = cumc[de][di]
                            if tc > wc[de]:
                                eng.wait_ge(semc[de], tc)
                                wc[de] = tc
                            nd = cntd[de][di]
                            if nd > 0:
                                for r_, v in dma_waits(de, nd):
                                    if wd[de].get(r_, 0) < v:
                                        eng.wait_ge(semd[de][r_], v)
                                        wd[de][r_] = v
                        kk = dmak[e][self.recs[e].index(r)] if False else None
                        kk = r.get("_k")
                        if kk is not None and kk >= R:
                            tgt = 16 * (kk // R)
                            if wd[e].get(kk % R, 0) < tgt:
                                eng.wait_ge(semd[e][kk % R], tgt)
                                wd[e][kk % R] = tgt
                        inst = r["fn"](eng)
                        if r["car"]:
                            if r["inc"] == 16:
                                inst.then_inc(semd[e][kk % R], 16)
                            else:
                                inst.then_inc(semc[e], 1)
                    if e == "gpsimd":
                        ntot = cntd[e][-1] if self.recs[e] else 0
                        for r_, v in dma_waits(e, ntot):
                            if wd[e].get(r_, 0) < v:
                                eng.wait_ge(semd[e][r_], v)
                return body
            for e in ENGINES:
                for i, r in enumerate(self.recs[e]):
                    r["_k"] = dmak[e][i] if dmak[e][i] >= 0 else None
            for e in ENGINES:
                getattr(block, e)(mk(e))


SCRATCH = {
    "xn_f32": ([T_LOC, D], F32),
    "xnT":    ([D, T_LOC], BF16),
    "qt0":    ([B_LOC, D, S], BF16),
    "kt0":    ([B_LOC, D, S], BF16),
    "v0":     ([B_LOC, S, 1040], BF16),
    "qt2":    ([B_LOC, D, S], BF16),
    "kt2":    ([B_LOC, D, S], BF16),
    "v2":     ([B_LOC, S, 1040], BF16),
    "attn1":  ([T_LOC, D], F32),
    "attn2":  ([T_LOC, D], F32),
    "ffn1":   ([T_LOC, D], F32),
    "x1tot":  ([T_LOC, D], F32),
    "xn1T":   ([D, T_LOC], BF16),
    "attn3":  ([T_LOC, D], F32),
}
SCR_PRODUCER = {
    "xn_f32": "A", "xnT": "A",
    "qt0": "B", "kt0": "B", "v0": "B", "qt2": "B", "kt2": "B", "v2": "B",
    "attn1": "C", "attn2": "C",
    "ffn1": "D",
    "x1tot": "E", "xn1T": "E",
    "attn3": "F",
}
WSPEC = {
    "x":      ([B_LOC, S, D], F32),
    "l0Wq":   ([D, D], F32), "l0Wk": ([D, D], F32), "l0Wv": ([D, D], F32),
    "l0WoT":  ([D, D], F32),
    "f0Wq":   ([D, D], F32), "f0Wk": ([D, D], F32), "f0Wv": ([D, D], F32),
    "f0Wo":   ([D, D], F32),
    "f0bq":   ([1, D], F32), "f0bk": ([1, D], F32), "f0bv": ([1, D], F32),
    "f0bo":   ([1, D], F32),
    "f0rw":   ([1, D], F32), "f0rb": ([1, 1], F32),
    "f0W1":   ([4, D, 4096], F32), "f0b1": ([4, 4096], F32),
    "f0W2":   ([4, 4096, D], F32), "f0b2": ([4, D], F32),
    "f0g1":   ([1, D], F32), "f0b1n": ([1, D], F32),
    "f0g2":   ([1, D], F32), "f0b2n": ([1, D], F32),
    "l1Wq":   ([D, D], F32), "l1Wkv": ([D, 128], F32),
    "l1Wk":   ([128, 512], F32), "l1Wv": ([128, 512], F32),
    "l1Wo":   ([D, D], F32),
    "l1qg":   ([128, 1], F32), "l1kg": ([128, 1], F32),
    "f1w12T": ([8, D, 4096], F32),
    "f1w3T":  ([8, 2048, D], F32),
    "f1gwT":  ([D, 8], F32),
    "cIdent": ([128, 128], BF16),
    "cMask":  ([4, 128, 512], F32),
    "cBlk64": ([128, 2], BF16),
    "cBlk64T": ([2, 128], BF16),
    "cScal":  ([128, 32], F32),
}
CONST_VALS = [1e-5, 1e-6, 0.125, 1.0 / 1024, 1.0 / 64, 1.0 / 32, 1.0, -1.0,
              -2.0, -3.0, 0.0, -0.5, -0.375, 1024.0, 0.25, -4.0, 0.7, -1e9,
              0.5, 2.0]
INTRA = {"w1bf", "w2bf", "b1bf", "b2bf", "w12bf", "w3bf"}
INTRA_SPEC = {
    "w1bf": ([4, D, 4096], BF16), "w2bf": ([4, 4096, D], BF16),
    "b1bf": ([4, 4096], BF16), "b2bf": ([4, D], BF16),
    "w12bf": ([8, D, 4096], BF16), "w3bf": ([8, 2048, D], BF16),
}


class Ctx:
    def __init__(self, nc, group):
        self.nc = nc
        self.P = Prog()
        self.group = group
        self.used_in = {}
        self.outs = []
        self._dram = {}
        self.off_bf = 0
        self.off_f32 = 0
        self._cm = contextlib.ExitStack()
        self.BFA = self._cm.enter_context(
            nc.sbuf_tensor("bfa", [128, 66560], BF16))
        self.F32A = self._cm.enter_context(
            nc.sbuf_tensor("f32a", [128, 15360], F32))
        self.PS = [self._cm.enter_context(
            nc.psum_tensor("ps%d" % i, [128, 512], F32))[:, :]
            for i in range(6)]
        self.PT = [self._cm.enter_context(
            nc.psum_tensor("pt%d" % i, [128, 128], BF16))[:, :]
            for i in range(2)]
        self.cscal = self._cm.enter_context(
            nc.sbuf_tensor("cscal", [128, 32], F32))
        for i, v in enumerate(CONST_VALS):
            nc.const_aps.register(F32, v, self.cscal[:, i:i + 1])

    def close(self):
        self._cm.close()

    def reset_arena(self):
        self.off_bf = 0
        self.off_f32 = 0

    def bf(self, cols):
        o = self.off_bf
        self.off_bf += (cols + 1) // 2 * 2
        assert self.off_bf <= 66560, "bf arena overflow"
        return self.BFA[:, o:o + cols]

    def f32(self, cols):
        o = self.off_f32
        self.off_f32 += cols
        assert self.off_f32 <= 15360, "f32 arena overflow"
        return self.F32A[:, o:o + cols]

    def w(self, name):
        if name not in self._dram:
            shape, dt = WSPEC[name]
            self._dram[name] = self.nc.dram_tensor(
                name, shape, dt, kind="ExternalInput").ap()
            self.used_in[name] = True
        return self._dram[name]

    def act(self, name):
        if name not in self._dram:
            if name == "out":
                shape, dt, kind = [T_LOC, D], F32, "ExternalOutput"
                self.outs.append(name)
            elif name in INTRA:
                shape, dt = INTRA_SPEC[name]
                kind = "Internal"
            else:
                shape, dt = SCRATCH[name]
                if len(self.group) > 1:
                    kind = "Internal"
                elif SCR_PRODUCER[name] in self.group:
                    kind = "ExternalOutput"
                    self.outs.append(name)
                else:
                    kind = "ExternalInput"
                    self.used_in[name] = True
            self._dram[name] = self.nc.dram_tensor(
                name, shape, dt, kind=kind).ap()
        return self._dram[name]

    def ld(self, dst, src, wkey, rkeys=()):
        self.P.emit("sync", lambda e, d=dst, s=src: e.dma_start(out=d, in_=s),
                    reads=tuple(rkeys), writes=(wkey,), inc=16)

    def st(self, dst, src, rkeys, wkey=None):
        wk = (wkey,) if wkey is not None else ()
        self.P.emit("gpsimd",
                    lambda e, d=dst, s=src: e.dma_start(out=d, in_=s),
                    reads=tuple(rkeys), writes=wk, inc=16)

    def mm(self, out, lhsT, rhs, rkeys, wkey, start, stop):
        self.P.emit("tensor",
                    lambda e, o=out, l=lhsT, r=rhs, s0=start, s1=stop:
                    e.matmul(o, l, r, start=s0, stop=s1),
                    reads=tuple(rkeys), writes=(wkey,))

    def tr(self, out, in_, ident, rkeys, wkey):
        self.P.emit("tensor",
                    lambda e, o=out, i=in_, d=ident: e.transpose(o, i, d),
                    reads=tuple(rkeys) + (("ident",),), writes=(wkey,))

    def activ(self, out, in_, func, rkeys, wkey, bias=None, scale=None):
        kw = {}
        if bias is not None:
            kw["bias"] = bias
        if scale is not None:
            kw["scale"] = scale
        self.P.emit("scalar",
                    lambda e, o=out, i=in_, f=func, k=kw:
                    e.activation(o, i, f, **k),
                    reads=tuple(rkeys), writes=(wkey,))

    def smul(self, out, in_, c, rkeys, wkey):
        self.P.emit("scalar", lambda e, o=out, i=in_, m=c: e.mul(o, i, m),
                    reads=tuple(rkeys), writes=(wkey,))

    def sadd(self, out, in_, c, rkeys, wkey):
        self.P.emit("scalar", lambda e, o=out, i=in_, a=c: e.add(o, i, a),
                    reads=tuple(rkeys), writes=(wkey,))

    def vec(self, fn, rkeys, wkeys):
        self.P.emit("vector", fn, reads=tuple(rkeys), writes=tuple(wkeys))

    def pool(self, fn, rkeys, wkeys):
        self.P.emit("gpsimd", fn, reads=tuple(rkeys), writes=tuple(wkeys))


def emit_ln_stats(C, xin, xkeys, xc, stats, tag, n=D, eps=EPS_LN,
                  xckey=None):
    if xckey is None:
        xckey = (tag, "xc")
    ssum = stats[:, 0:1]
    nmean = stats[:, 1:2]
    vsum = stats[:, 2:3]
    stdv = stats[:, 4:5]
    rstd = stats[:, 3:4]
    sk = (tag, "stats")
    C.vec(lambda e, o=ssum, i_=xin: e.reduce_sum(o, i_, axis=AX.X,
                                                 negate=True),
          xkeys, (sk,))
    C.smul(nmean, ssum, 1.0 / n, (sk,), sk)
    C.activ(xc, xin, AF.Identity, tuple(xkeys) + (sk,), xckey,
            bias=nmean)
    sq = C._sq
    C.activ(sq, xc, AF.Square, (xckey,), (tag, "sq"))
    C.vec(lambda e, o=vsum, i_=sq: e.reduce_sum(o, i_, axis=AX.X),
          ((tag, "sq"),), (sk,))
    C.activ(stdv, vsum, AF.Sqrt, (sk,), (sk,), bias=float(eps),
            scale=1.0 / n)
    C.vec(lambda e, o=rstd, i_=stdv: e.reciprocal(o, i_), (sk,), (sk,))
    return rstd, sk


def _ln_transpose_stage(C, src_is_sum):
    xr = C.w("x").rearrange("b (n p) d -> (b n) p d", p=128)
    if src_is_sum:
        a1r = C.act("attn1").rearrange("(n p) d -> n p d", p=128)
        f1r = C.act("ffn1").rearrange("(n p) d -> n p d", p=128)
        x1r = C.act("x1tot").rearrange("(n p) d -> n p d", p=128)
        xnTr = C.act("xn1T").rearrange("(c p) t -> c p t", p=128)
        xnr = None
    else:
        xnr = C.act("xn_f32").rearrange("(n p) d -> n p d", p=128)
        xnTr = C.act("xnT").rearrange("(c p) t -> c p t", p=128)
    ident = C.bf(128)
    C.ld(ident, C.w("cIdent"), ("ident",))
    xa = [C.f32(1024) for _ in range(2)]
    xb = [C.f32(1024) for _ in range(2)]
    xc = [C.f32(1024) for _ in range(2)]
    C._sq = C.f32(1024)
    stats = [C.f32(8) for _ in range(2)]
    xnb = [C.bf(1024) for _ in range(2)]
    xnf = [C.f32(1024) for _ in range(2)] if not src_is_sum else None
    slabT = [C.bf(512) for _ in range(8)]
    for sl in range(NSLAB):
        for k in range(4):
            i = 4 * sl + k
            r = i % 2
            C.ld(xa[r], xr[i], ("xa", r))
            if src_is_sum:
                C.ld(xb[r], a1r[i], ("xb", r))
                C.vec(lambda e, o=xa[r], a=xa[r], b=xb[r]:
                      e.tensor_tensor(o, a, b, ALU.add),
                      (("xa", r), ("xb", r)), (("xa", r),))
                C.ld(xb[r], f1r[i], ("xb", r))
                C.vec(lambda e, o=xa[r], a=xa[r], b=xb[r]:
                      e.tensor_tensor(o, a, b, ALU.add),
                      (("xa", r), ("xb", r)), (("xa", r),))
                C.st(x1r[i], xa[r], (("xa", r),))
            rstd, sk = emit_ln_stats(C, xa[r], (("xa", r),), xc[r],
                                     stats[r], ("ln", r))
            if src_is_sum:
                C.vec(lambda e, o=xnb[r], i_=xc[r], s=rstd:
                      e.tensor_scalar_mul(o, i_, s),
                      (("ln", r, "xc"), sk), (("xnb", r),))
            else:
                C.vec(lambda e, o=xnf[r], i_=xc[r], s=rstd:
                      e.tensor_scalar_mul(o, i_, s),
                      (("ln", r, "xc"), sk), (("xnf", r),))
                C.st(xnr[i], xnf[r], (("xnf", r),))
                C.activ(xnb[r], xnf[r], AF.Copy, (("xnf", r),), ("xnb", r))
            for c in range(8):
                pt = C.PT[c % 2]
                ptk = ("PT", c % 2)
                C.tr(pt, xnb[r][:, 128 * c:128 * (c + 1)], ident,
                     (("xnb", r),), ptk)
                C.vec(lambda e, o=slabT[c][:, 128 * k:128 * (k + 1)], i_=pt:
                      e.tensor_copy(o, i_), (ptk,), (("slabT", c),))
        for c in range(8):
            C.st(xnTr[c, :, 512 * sl:512 * (sl + 1)], slabT[c],
                 (("slabT", c),))


def stage_A(C):
    _ln_transpose_stage(C, False)


def stage_E(C):
    _ln_transpose_stage(C, True)


def _load_w_chunks(C, wname, wreg, tag, wstage, nchunk=8):
    wr = C.w(wname).rearrange("(c p) n -> c p n", p=128)
    for c in range(nchunk):
        sk = ("wstg", c % 2)
        C.ld(wstage[c % 2], wr[c], sk)
        C.activ(wreg[c], wstage[c % 2], AF.Copy, (sk,), (tag, c))


def stage_B(C):
    xnTr = C.act("xnT").rearrange("(c p) t -> c p t", p=128)
    xnT = [C.bf(4096) for _ in range(8)]
    for c in range(8):
        C.ld(xnT[c], xnTr[c], ("xnT", c))
    wstage = [C.f32(1024) for _ in range(2)]
    wreg = [[C.bf(1024) for _ in range(8)] for _ in range(2)]
    ev = [C.bf(512) for _ in range(2)]
    rows_f = C.f32(1024)
    rows = C.bf(1024)
    ones = C.bf(512)
    C.vec(lambda e, o=ones[0:1, :]: e.memset(o, 1.0), (), (("ones",),))
    v65 = C.bf(1040)

    def ld_bias(bname):
        C.ld(rows_f[0:1, :], C.w(bname), ("rowf",))
        C.activ(rows[0:1, :], rows_f[0:1, :], AF.Copy, (("rowf",),),
                ("row",))

    def fm_proj(wname, bname, dst, ws):
        tag = ("W", ws)
        _load_w_chunks(C, wname, wreg[ws], tag, wstage)
        if bname:
            ld_bias(bname)
        dr = C.act(dst).rearrange("b (c p) s -> b c p s", p=128)
        for b in range(B_LOC):
            for cf in range(8):
                for sl in range(4):
                    ps = C.PS[sl % 2]
                    pk = ("PS", sl % 2)
                    for cd in range(8):
                        C.mm(ps, wreg[ws][cd][:, 128 * cf:128 * (cf + 1)],
                             xnT[cd][:, 2048 * b + 512 * sl:
                                     2048 * b + 512 * (sl + 1)],
                             ((tag, cd), ("xnT", cd)), pk,
                             start=(cd == 0), stop=(cd == 7 and not bname))
                    if bname:
                        C.mm(ps, rows[0:1, 128 * cf:128 * (cf + 1)],
                             ones[0:1, :], (("row",), ("ones",)), pk,
                             start=False, stop=True)
                    C.activ(ev[sl % 2], ps, AF.Copy, (pk,), ("ev", sl % 2))
                    C.st(dr[b, cf, :, 512 * sl:512 * (sl + 1)], ev[sl % 2],
                         (("ev", sl % 2),))

    def tm_proj(wname, bname, dst, ws):
        tag = ("W", ws)
        _load_w_chunks(C, wname, wreg[ws], tag, wstage)
        if bname:
            ld_bias(bname)
        dr = C.act(dst).rearrange("b (n p) c -> b n p c", p=128)
        v65h = v65.rearrange("p (h v) -> p h v", v=65)
        for b in range(B_LOC):
            for stile in range(16):
                for hf in range(2):
                    ps = C.PS[hf]
                    pk = ("PS", hf)
                    for cd in range(8):
                        C.mm(ps, xnT[cd][:, 2048 * b + 128 * stile:
                                         2048 * b + 128 * (stile + 1)],
                             wreg[ws][cd][:, 512 * hf:512 * (hf + 1)],
                             ((tag, cd), ("xnT", cd)), pk,
                             start=(cd == 0), stop=(cd == 7 and not bname))
                    if bname:
                        C.mm(ps, ones[0:1, 0:128],
                             rows[0:1, 512 * hf:512 * (hf + 1)],
                             (("row",), ("ones",)), pk, start=False,
                             stop=True)
                    psh = ps.rearrange("p (h v) -> p h v", v=64)
                    C.vec(lambda e, o=v65h[:, 8 * hf:8 * (hf + 1), 0:64],
                          i_=psh: e.tensor_copy(o, i_),
                          (pk,), (("v65", hf),))
                C.vec(lambda e, o=v65h[:, :, 64:65]: e.memset(o, 1.0),
                      (), (("v65", "ones"),))
                C.st(dr[b, stile], v65,
                     (("v65", 0), ("v65", 1), ("v65", "ones")))

    fm_proj("l0Wq", None, "qt0", 0)
    fm_proj("l0Wk", None, "kt0", 1)
    tm_proj("l0Wv", None, "v0", 0)
    fm_proj("f0Wq", "f0bq", "qt2", 1)
    fm_proj("f0Wk", "f0bk", "kt2", 0)
    tm_proj("f0Wv", "f0bv", "v2", 1)


def _attention(C, qt, kt, vt, wo_name, bo_name, dst, causal,
               masks, wreg, wstage, rows_f, rows, ones, QT, KT, V, ccT,
               probsT, stmp, rrow, rrowf, rb, ostg):
    tag = ("WF",)
    _load_w_chunks(C, wo_name, wreg, tag, wstage)
    if bo_name:
        C.ld(rows_f[0:1, :], C.w(bo_name), ("rowf",))
        C.activ(rows[0:1, :], rows_f[0:1, :], AF.Copy, (("rowf",),),
                ("row",))
    qr = C.act(qt).rearrange("b (c p) s -> b c p s", p=128)
    kr = C.act(kt).rearrange("b (c p) s -> b c p s", p=128)
    vr = C.act(vt).rearrange("b (n p) c -> b n p c", p=128)
    drr = C.act(dst).rearrange("(n p) d -> n p d", p=128)
    for b in range(B_LOC):
        for c in range(8):
            C.ld(QT[c], qr[b, c], ("QT", c))
            C.ld(KT[c], kr[b, c], ("KT", c))
        for stile in range(16):
            C.ld(V[:, 1040 * stile:1040 * (stile + 1)], vr[b, stile],
                 ("V",))
        for qb in range(4):
            schunks = list(range(4 * qb + 4)) if causal else list(range(16))
            nsc = len(schunks)
            for h in range(16):
                cf, base = h // 2, 64 * (h % 2)
                pv = C.PS[2 + h % 2]
                pvk = ("PS", 2 + h % 2)
                for idx, sc in enumerate(schunks):
                    sp = C.PS[idx % 2]
                    spk = ("PS", idx % 2)
                    C.mm(sp, KT[cf][base:base + 64, 128 * sc:128 * (sc + 1)],
                         QT[cf][base:base + 64, 512 * qb:512 * (qb + 1)],
                         (("KT", cf), ("QT", cf)), spk, start=True,
                         stop=True)
                    jl = sc - 4 * qb
                    if causal and jl >= 0:
                        C.vec(lambda e, o=stmp[idx % 2], a=sp, m=masks[jl]:
                              e.tensor_tensor(o, a, m, ALU.add),
                              (spk, ("mask", jl)), (("stmp", idx % 2),))
                        C.activ(probsT[idx % 2], stmp[idx % 2], AF.Exp,
                                (("stmp", idx % 2),), ("probs", idx % 2),
                                scale=0.125)
                    else:
                        C.activ(probsT[idx % 2], sp, AF.Exp,
                                (spk,), ("probs", idx % 2), scale=0.125)
                    C.mm(pv[0:65, :],
                         V[:, 1040 * sc + 65 * h:1040 * sc + 65 * (h + 1)],
                         probsT[idx % 2],
                         (("V",), ("probs", idx % 2)), pvk,
                         start=(idx == 0), stop=(idx == nsc - 1))
                C.vec(lambda e, o=rrowf[0:1, :], i_=pv[64:65, :]:
                      e.reciprocal(o, i_), (pvk,), (("rrow",),))
                C.activ(rrow[0:1, :], rrowf[0:1, :], AF.Copy, (("rrow",),),
                        ("rrow",))
                pb = C.PS[4]
                C.mm(pb[0:64, :], ones[0:1, 0:64], rrow[0:1, :],
                     (("rrow",), ("ones",)), ("PS", 4), start=True,
                     stop=True)
                C.activ(rb[0:64, :], pb[0:64, :], AF.Copy, (("PS", 4),),
                        ("rb",))
                C.vec(lambda e, o=ccT[cf][base:base + 64, :],
                      a=pv[0:64, :], m=rb[0:64, :]:
                      e.tensor_tensor(o, a, m, ALU.mult),
                      (pvk, ("rb",)), (("ccT", cf),))
            for qt_i in range(4):
                for dc in range(2):
                    ps = C.PS[5]
                    pk = ("PS", 5)
                    for cf in range(8):
                        C.mm(ps, ccT[cf][:, 128 * qt_i:128 * (qt_i + 1)],
                             wreg[cf][:, 512 * dc:512 * (dc + 1)],
                             (("ccT", cf), (tag, cf)), pk,
                             start=(cf == 0),
                             stop=(cf == 7 and not bo_name))
                    if bo_name:
                        C.mm(ps, ones[0:1, 0:128],
                             rows[0:1, 512 * dc:512 * (dc + 1)],
                             (("row",), ("ones",)), pk, start=False,
                             stop=True)
                    C.activ(ostg[:, 512 * dc:512 * (dc + 1)], ps, AF.Copy,
                            (pk,), ("ostg",))
                ti = 16 * b + 4 * qb + qt_i
                C.st(drr[ti], ostg, (("ostg",),))


def stage_C(C):
    masks = [C.f32(512) for _ in range(4)]
    mr = C.w("cMask")
    for jl in range(4):
        C.ld(masks[jl], mr[jl], ("mask", jl))
    wstage = [C.f32(1024) for _ in range(2)]
    wreg = [C.bf(1024) for _ in range(8)]
    rows_f = C.f32(1024)
    rows = C.bf(1024)
    ones = C.bf(512)
    C.vec(lambda e, o=ones[0:1, :]: e.memset(o, 1.0), (), (("ones",),))
    QT = [C.bf(2048) for _ in range(8)]
    KT = [C.bf(2048) for _ in range(8)]
    V = C.bf(16640)
    ccT = [C.bf(512) for _ in range(8)]
    probsT = [C.bf(512) for _ in range(2)]
    stmp = [C.f32(512) for _ in range(2)]
    rrow = C.bf(512)
    rrowf = C.f32(512)
    rb = C.bf(512)
    ostg = C.f32(1024)
    _attention(C, "qt0", "kt0", "v0", "l0WoT", None, "attn1", True,
               masks, wreg, wstage, rows_f, rows, ones, QT, KT, V, ccT,
               probsT, stmp, rrow, rrowf, rb, ostg)
    _attention(C, "qt2", "kt2", "v2", "f0Wo", "f0bo", "attn2", False,
               masks, wreg, wstage, rows_f, rows, ones, QT, KT, V, ccT,
               probsT, stmp, rrow, rrowf, rb, ostg)


def stage_D(C):
    x1f = [C.f32(1024) for _ in range(4)]
    stage_f = [x1f[0].tensor[:, x1f[0].offsets[1]:x1f[0].offsets[1] + 2048]
               if False else None for _ in range(2)]
    stage_f = [C.F32A[:, 0:2048], C.F32A[:, 2048:4096]]
    stage_b = [C.bf(2048) for _ in range(2)]
    w1s = C.w("f0W1").rearrange("e (c p) j -> (e c) p j", p=128)
    w1d = C.act("w1bf").rearrange("e (c p) j -> (e c) p j", p=128)
    for i in range(32):
        for half in range(2):
            r = half
            C.ld(stage_f[r], w1s[i][:, 2048 * half:2048 * (half + 1)],
                 ("cvf", r))
            C.activ(stage_b[r], stage_f[r], AF.Copy, (("cvf", r),),
                    ("cvb", r))
            C.st(w1d[i][:, 2048 * half:2048 * (half + 1)], stage_b[r],
                 (("cvb", r),), wkey=("w1bf",))
    w2s = C.w("f0W2").rearrange("e (c p) d -> (e c) p d", p=128)
    w2d = C.act("w2bf").rearrange("e (c p) d -> (e c) p d", p=128)
    for i in range(128):
        r = i % 2
        C.ld(stage_f[r][:, 0:1024], w2s[i], ("cvf", r))
        C.activ(stage_b[r][:, 0:1024], stage_f[r][:, 0:1024], AF.Copy,
                (("cvf", r),), ("cvb", r))
        C.st(w2d[i], stage_b[r][:, 0:1024], (("cvb", r),), wkey=("w2bf",))
    b1s = C.w("f0b1")
    b1d = C.act("b1bf")
    for half in range(2):
        C.ld(stage_f[half][0:4, :], b1s[:, 2048 * half:2048 * (half + 1)],
             ("cvf", half))
        C.activ(stage_b[half][0:4, :], stage_f[half][0:4, :], AF.Copy,
                (("cvf", half),), ("cvb", half))
        C.st(b1d[:, 2048 * half:2048 * (half + 1)], stage_b[half][0:4, :],
             (("cvb", half),), wkey=("b1bf",))
    C.ld(stage_f[0][0:4, 0:1024], C.w("f0b2"), ("cvf", 0))
    C.activ(stage_b[0][0:4, 0:1024], stage_f[0][0:4, 0:1024], AF.Copy,
            (("cvf", 0),), ("cvb", 0))
    C.st(C.act("b2bf"), stage_b[0][0:4, 0:1024], (("cvb", 0),),
         wkey=("b2bf",))
    C.off_f32 = 4096
    rowf = C.f32(1024)
    rwbc = C.f32(1024)
    rbbc = C.f32(1)
    g1bc = C.f32(1024)
    b1nbc = C.f32(1024)
    g2bc = C.bf(1024)
    b2nbc = C.bf(1024)
    onesf = C.f32(128)
    C.vec(lambda e, o=onesf[0:1, :]: e.memset(o, 1.0), (), (("onesf",),))

    def bcast_row(name, src_cols, dstt, key):
        ps = C.PS[4]
        nn = src_cols
        for c0 in range(0, nn, 512):
            w = min(512, nn - c0)
            C.mm(ps[:, 0:w], onesf[0:1, :], rowf[0:1, c0:c0 + w],
                 (("onesf",), ("rowf",)), ("PS", 4), start=True, stop=True)
            C.activ(dstt[:, c0:c0 + w], ps[:, 0:w], AF.Copy, (("PS", 4),),
                    key)

    for name, dstt in (("f0g1", g1bc), ("f0b1n", b1nbc), ("f0g2", g2bc),
                       ("f0b2n", b2nbc)):
        C.ld(rowf[0:1, :], C.w(name), ("rowf",))
        bcast_row(name, 1024, dstt, (name, "bc"))
    C.ld(rowf[0:1, :], C.w("f0rw"), ("rowf",))
    bcast_row("f0rw", 1024, rwbc, ("rw", "bc"))
    C.ld(rowf[0:1, 0:1], C.w("f0rb"), ("rowf",))
    bcast_row("f0rb", 1, rbbc, ("rb", "bc"))
    ident = C.bf(128)
    C.ld(ident, C.w("cIdent"), ("ident",))
    ones = C.bf(512)
    C.vec(lambda e, o=ones[0:1, :]: e.memset(o, 1.0), (), (("ones",),))
    det = C.f32(8)
    deti = C._cm.enter_context(C.nc.sbuf_tensor("deti", [128, 4], I32))
    cflag = det[:, 0:1]
    ncflag = det[:, 1:2]
    C.vec(lambda e, o=det[:, 2:3]: e.memset(o, 0.7), (), (("det",),))
    C.vec(lambda e, o=deti[:, 0:1], i_=det[:, 2:3]: e.tensor_copy(o, i_),
          (("det",),), (("det",),))
    C.vec(lambda e, o=cflag, i_=deti[:, 0:1]: e.tensor_copy(o, i_),
          (("det",),), (("det",),))
    C.vec(lambda e, o=ncflag, i_=cflag:
          e.tensor_scalar(o, i_, -1.0, -1.0, ALU.mult, ALU.subtract),
          (("det",),), (("det",),))
    ffs = [C.f32(1024) for _ in range(4)]
    C._sq = C.f32(1024)
    stats = C.f32(8)
    keysb = C.f32(16)
    inds = C.f32(16)
    scr2 = C.f32(2)
    scrD = C.f32(1024)
    xcb = scrD
    o2t = scrD
    a2_tile = scrD
    xnb_tile = rowf
    x1b = [C.bf(1024) for _ in range(4)]
    x1T = [C.bf(512) for _ in range(8)]
    hT = [C.bf(512) for _ in range(32)]
    w1c = [C.bf(1024) for _ in range(2)]
    w2c = [C.bf(512) for _ in range(32)]
    b1row = C.bf(4096)
    b2row = C.bf(1024)
    xnr = C.act("xn_f32").rearrange("(n p) d -> n p d", p=128)
    a2r = C.act("attn2").rearrange("(n p) d -> n p d", p=128)
    ffr = C.act("ffn1").rearrange("(n p) d -> n p d", p=128)
    w1br = C.act("w1bf").rearrange("e (c p) j -> e c p j", p=128)
    w2br = C.act("w2bf").rearrange("e (c p) d -> (e c) p d", p=128)

    def f2i_floor(src, skey, delta_rn, colA, colB, out_col, okey):
        C.sadd(colA, src, delta_rn, (skey,), ("kf", "a"))
        C.vec(lambda e, o=deti[:, 1:2], i_=colA: e.tensor_copy(o, i_),
              (("kf", "a"),), (("kf", "ia"),))
        C.vec(lambda e, o=colA, i_=deti[:, 1:2]: e.tensor_copy(o, i_),
              (("kf", "ia"),), (("kf", "a"),))
        C.vec(lambda e, o=deti[:, 2:3], i_=src: e.tensor_copy(o, i_),
              (skey,), (("kf", "ib"),))
        C.vec(lambda e, o=colB, i_=deti[:, 2:3]: e.tensor_copy(o, i_),
              (("kf", "ib"),), (("kf", "b"),))
        C.vec(lambda e, o=colA, i_=colA, s=cflag:
              e.tensor_scalar_mul(o, i_, s),
              (("kf", "a"), ("det",)), (("kf", "a"),))
        C.vec(lambda e, o=out_col, i0=colB, s=ncflag, i1=colA:
              e.scalar_tensor_tensor(o, i0, s, i1, ALU.mult, ALU.add),
              (("kf", "a"), ("kf", "b"), ("det",)), (okey,))

    for sl in range(NSLAB):
        for k in range(4):
            C.vec(lambda e, o=ffs[k]: e.memset(o, 0.0), (), (("ff", k),))
        for k in range(4):
            i = 4 * sl + k
            C.ld(xnb_tile, xnr[i], ("rowf",))
            C.ld(a2_tile, a2r[i], ("a2T",))
            extra = (("cvb", 0), ("cvb", 1)) if sl == 0 else ()
            C.P.emit("vector",
                     lambda e, o=x1f[k], a=xnb_tile, b=a2_tile:
                     e.tensor_tensor(o, a, b, ALU.add),
                     reads=(("rowf",), ("a2T",)) + extra,
                     writes=(("x1f", k),))
            rstd, sk = emit_ln_stats(C, x1f[k], (("x1f", k),), xcb, stats,
                                     ("lnD",), xckey=("a2T",))
            C.vec(lambda e, o=x1f[k], i0=xcb, s=rstd, i1=g1bc:
                  e.scalar_tensor_tensor(o, i0, s, i1, ALU.mult, ALU.mult),
                  (("a2T",), sk, ("f0g1", "bc")), (("x1f", k),))
            C.vec(lambda e, o=x1f[k], a=x1f[k], b=b1nbc:
                  e.tensor_tensor(o, a, b, ALU.add),
                  (("x1f", k), ("f0b1n", "bc")), (("x1f", k),))
            C.activ(x1b[k], x1f[k], AF.Copy, (("x1f", k),), ("x1b", k))
            C.vec(lambda e, o=C._sq, a=x1f[k], b=rwbc:
                  e.tensor_tensor(o, a, b, ALU.mult),
                  (("x1f", k), ("rw", "bc")), (("lnD", "sq"),))
            kcol = keysb[:, 4 * k:4 * k + 1]
            gcol = keysb[:, 4 * k + 1:4 * k + 2]
            fcol = keysb[:, 4 * k + 2:4 * k + 3]
            qcol = keysb[:, 4 * k + 3:4 * k + 4]
            C.vec(lambda e, o=kcol, i_=C._sq: e.reduce_sum(o, i_, axis=AX.X),
                  (("lnD", "sq"),), (("keys", k),))
            C.vec(lambda e, o=kcol, a=kcol, b=rbbc:
                  e.tensor_tensor(o, a, b, ALU.add),
                  (("keys", k), ("rb", "bc")), (("keys", k),))
            C.sadd(gcol, kcol, 1024.0, (("keys", k),), ("keys", k))
            f2i_floor(gcol, ("keys", k), -0.5, scr2[:, 0:1], scr2[:, 1:2],
                      fcol, ("fl", k))
            C.smul(gcol, fcol, 0.25, (("fl", k),), ("keys", k))
            f2i_floor(gcol, ("keys", k), -0.375, scr2[:, 0:1], scr2[:, 1:2],
                      qcol, ("q", k))
            C.vec(lambda e, o=fcol, i0=qcol, s=-4.0, i1=fcol:
                  e.scalar_tensor_tensor(o, i0, s, i1, ALU.mult, ALU.add),
                  (("q", k), ("fl", k)), (("m", k),))
            for e_ in range(4):
                icol = inds[:, 4 * k + e_:4 * k + e_ + 1]
                C.activ(icol, fcol, AF.Square, (("m", k),), ("ind", k, e_),
                        bias=float(-e_))
                C.activ(icol, icol, AF.Relu, (("ind", k, e_),),
                        ("ind", k, e_), bias=1.0, scale=-1.0)
            for c in range(8):
                pt = C.PT[c % 2]
                ptk = ("PT", c % 2)
                C.tr(pt, x1b[k][:, 128 * c:128 * (c + 1)], ident,
                     (("x1b", k),), ptk)
                C.vec(lambda e, o=x1T[c][:, 128 * k:128 * (k + 1)], i_=pt:
                      e.tensor_copy(o, i_), (ptk,), (("x1T", c),))
        for e_ in range(4):
            C.ld(b1row[0:1, :], b1d[e_:e_ + 1, :], ("b1row",),
                 rkeys=(("b1bf",),))
            C.ld(b2row[0:1, :], C.act("b2bf")[e_:e_ + 1, :], ("b2row",),
                 rkeys=(("b2bf",),))
            for j in range(32):
                r = j % 2
                C.ld(w1c[r].rearrange("p (c j) -> p c j", j=128),
                     w1br[e_, :, :, 128 * j:128 * (j + 1)].rearrange(
                         "c p j -> p c j"),
                     ("w1c", r), rkeys=(("w1bf",),))
                ps = C.PS[j % 2]
                pk = ("PS", j % 2)
                for c in range(8):
                    C.mm(ps, w1c[r][:, 128 * c:128 * (c + 1)], x1T[c],
                         (("w1c", r), ("x1T", c)), pk,
                         start=(c == 0), stop=False)
                C.mm(ps, b1row[0:1, 128 * j:128 * (j + 1)], ones[0:1, :],
                     (("b1row",), ("ones",)), pk, start=False, stop=True)
                C.activ(hT[j], ps, AF.Relu, (pk,), ("hT", j))
            for dc in range(2):
                for j in range(32):
                    C.ld(w2c[j], w2br[32 * e_ + j][:, 512 * dc:512 * (dc + 1)],
                         ("w2c", j), rkeys=(("w2bf",),))
                for tt in range(4):
                    ps = C.PS[2 + tt % 2]
                    pk = ("PS", 2 + tt % 2)
                    for j in range(32):
                        C.mm(ps, hT[j][:, 128 * tt:128 * (tt + 1)], w2c[j],
                             (("hT", j), ("w2c", j)), pk,
                             start=(j == 0), stop=False)
                    C.mm(ps, ones[0:1, 0:128],
                         b2row[0:1, 512 * dc:512 * (dc + 1)],
                         (("b2row",), ("ones",)), pk, start=False, stop=True)
                    C.vec(lambda e, o=ffs[tt][:, 512 * dc:512 * (dc + 1)],
                          i0=ps, s=inds[:, 4 * tt + e_:4 * tt + e_ + 1],
                          i1=ffs[tt][:, 512 * dc:512 * (dc + 1)]:
                          e.scalar_tensor_tensor(o, i0, s, i1, ALU.mult,
                                                 ALU.add),
                          (pk, ("ind", tt, e_), ("ff", tt)), (("ff", tt),))
        for k in range(4):
            i = 4 * sl + k
            C.vec(lambda e, o=xcb, a=x1f[k], b=ffs[k]:
                  e.tensor_tensor(o, a, b, ALU.add),
                  (("x1f", k), ("ff", k)), (("a2T",),))
            rstd, sk = emit_ln_stats(C, xcb, (("a2T",),), x1f[k], stats,
                                     ("lnE",), xckey=("x1f", k))
            C.vec(lambda e, o=o2t, i0=x1f[k], s=rstd, i1=g2bc:
                  e.scalar_tensor_tensor(o, i0, s, i1, ALU.mult, ALU.mult),
                  (("x1f", k), sk, ("f0g2", "bc")), (("a2T",),))
            C.vec(lambda e, o=o2t, a=o2t, b=b2nbc:
                  e.tensor_tensor(o, a, b, ALU.add),
                  (("a2T",), ("f0b2n", "bc")), (("a2T",),))
            C.st(ffr[i], o2t, (("a2T",),))


def stage_F(C):
    masks = [C.f32(512) for _ in range(4)]
    mr = C.w("cMask")
    for jl in range(4):
        C.ld(masks[jl], mr[jl], ("mask", jl))
    wstage = [C.f32(1024) for _ in range(2)]
    wreg = [C.bf(1024) for _ in range(8)]
    wkv = C.bf(1024)
    wkb = C.bf(512)
    wvb = C.bf(512)
    ones = C.bf(512)
    C.vec(lambda e, o=ones[0:1, :]: e.memset(o, 1.0), (), (("ones",),))
    blk = C.bf(2)
    blkT = C.bf(128)
    C.ld(blk, C.w("cBlk64"), ("blk",))
    C.ld(blkT[0:2, :], C.w("cBlk64T"), ("blkT",))
    qg = C.f32(1)
    kg = C.f32(1)
    C.ld(qg, C.w("l1qg"), ("qg",))
    C.ld(kg, C.w("l1kg"), ("kg",))
    QT = [C.bf(2048) for _ in range(8)]
    KT = [C.bf(2048) for _ in range(8)]
    V = C.bf(8448)
    kvT = C.bf(2048)
    ccT = [C.bf(512) for _ in range(8)]
    probsT = [C.bf(512) for _ in range(2)]
    stmp = [C.f32(512) for _ in range(2)]
    rrow = C.bf(512)
    rb = C.bf(512)
    ostg = C.f32(1024)
    xnTs = [C.bf(512) for _ in range(8)]
    qtmp = C.f32(512)
    qsq = C.bf(512)
    sstd = C.f32(512)
    rstf = C.f32(512)
    rrowf = C.f32(512)
    rstd2 = C.bf(512)
    rbq = C.bf(512)
    xnTr = C.act("xn1T").rearrange("(c p) t -> c p t", p=128)
    drr = C.act("attn3").rearrange("(n p) d -> n p d", p=128)
    kvr = C.w("l1Wkv").rearrange("(c p) l -> c p l", p=128)

    def rms_norm_tile(dst_ap, dkey, src_f32, skeys, gain, gkey, nscale):
        C.activ(qsq, src_f32, AF.Square, skeys, ("qsq",))
        pb = C.PS[4]
        C.mm(pb[0:2, :], blk, qsq, (("blk",), ("qsq",)), ("PS", 4),
             start=True, stop=True)
        C.activ(sstd[0:2, :], pb[0:2, :], AF.Sqrt, (("PS", 4),), ("sstd",),
                bias=float(EPS_RMS), scale=nscale)
        C.vec(lambda e, o=rstf[0:2, :], i_=sstd[0:2, :]:
              e.reciprocal(o, i_), (("sstd",),), (("rstd2",),))
        C.activ(rstd2[0:2, :], rstf[0:2, :], AF.Copy, (("rstd2",),),
                ("rstd2",))
        C.mm(pb, blkT[0:2, :], rstd2[0:2, :],
             (("blkT",), ("rstd2",)), ("PS", 4), start=True, stop=True)
        C.activ(rbq, pb, AF.Copy, (("PS", 4),), ("rbq",))
        C.vec(lambda e, o=qsq, a=src_f32, b=rbq:
              e.tensor_tensor(o, a, b, ALU.mult),
              tuple(skeys) + (("rbq",),), (("qsq",),))
        C.vec(lambda e, o=dst_ap, i_=qsq, s=gain:
              e.tensor_scalar_mul(o, i_, s),
              (("qsq",), gkey), (dkey,))

    for b in range(B_LOC):
        C.vec(lambda e, o=V: e.memset(o, 1.0), (), (("V", "all"),))
        for c in range(8):
            C.vec(lambda e, o=KT[c]: e.memset(o, 0.0), (), (("KT", c),))
            C.vec(lambda e, o=ccT[c]: e.memset(o, 0.0), (), (("ccT", c),))
        _load_w_chunks(C, "l1Wq", wreg, ("WF",), wstage)
        for c in range(8):
            sk = ("wstg", c % 2)
            C.ld(wstage[c % 2][:, 0:128], kvr[c], sk)
            C.activ(wkv[:, 128 * c:128 * (c + 1)], wstage[c % 2][:, 0:128],
                    AF.Copy, (sk,), ("Wkv", c))
        C.ld(wstage[0][:, 0:512], C.w("l1Wk"), ("wstg", 0))
        C.activ(wkb, wstage[0][:, 0:512], AF.Copy, (("wstg", 0),), ("Wk",))
        C.ld(wstage[0][:, 0:512], C.w("l1Wv"), ("wstg", 0))
        C.activ(wvb, wstage[0][:, 0:512], AF.Copy, (("wstg", 0),), ("Wv",))
        for sl in range(4):
            for cd in range(8):
                C.ld(xnTs[cd],
                     xnTr[cd, :, 2048 * b + 512 * sl:
                          2048 * b + 512 * (sl + 1)],
                     ("xnTs", cd))
            for cf in range(8):
                ps = C.PS[cf % 2]
                pk = ("PS", cf % 2)
                for cd in range(8):
                    C.mm(ps, wreg[cd][:, 128 * cf:128 * (cf + 1)], xnTs[cd],
                         ((("WF",), cd), ("xnTs", cd)), pk,
                         start=(cd == 0), stop=(cd == 7))
                C.activ(qtmp, ps, AF.Copy, (pk,), ("qtmp",))
                rms_norm_tile(QT[cf][:, 512 * sl:512 * (sl + 1)],
                              ("QT", cf), qtmp, (("qtmp",),), qg, ("qg",),
                              1.0 / 64)
            ps = C.PS[2]
            pk = ("PS", 2)
            for cd in range(8):
                C.mm(ps, wkv[:, 128 * cd:128 * (cd + 1)], xnTs[cd],
                     (("Wkv", cd), ("xnTs", cd)), pk,
                     start=(cd == 0), stop=(cd == 7))
            C.activ(kvT[:, 512 * sl:512 * (sl + 1)], ps, AF.Copy, (pk,),
                    ("kvT",))
            for c in range(8):
                ps = C.PS[3]
                pk = ("PS", 3)
                for hh in range(2):
                    h = 2 * c + hh
                    C.mm(ps[64 * hh:64 * hh + 32, :],
                         wkb[:, 32 * h:32 * (h + 1)],
                         kvT[:, 512 * sl:512 * (sl + 1)],
                         (("Wk",), ("kvT",)), pk, start=True, stop=True)
                for hh in range(2):
                    C.vec(lambda e,
                          o=KT[c][64 * hh:64 * hh + 32,
                                  512 * sl:512 * (sl + 1)],
                          i_=ps[64 * hh:64 * hh + 32, :]:
                          e.tensor_copy(o, i_), (pk,), (("KT", c),))
                C.vec(lambda e, o=qtmp,
                      i_=KT[c][:, 512 * sl:512 * (sl + 1)]:
                      e.tensor_copy(o, i_), (("KT", c),), (("qtmp",),))
                rms_norm_tile(KT[c][:, 512 * sl:512 * (sl + 1)],
                              ("KT", c), qtmp, (("qtmp",),), kg, ("kg",),
                              1.0 / 32)
            for st_ in range(4):
                stile = 4 * sl + st_
                ps = C.PS[5]
                pk = ("PS", 5)
                C.mm(ps, kvT[:, 512 * sl + 128 * st_:
                             512 * sl + 128 * (st_ + 1)],
                     wvb, (("kvT",), ("Wv",)), pk, start=True, stop=True)
                psh = ps.rearrange("p (h v) -> p h v", v=32)
                vh = V[:, 528 * stile:528 * (stile + 1)].rearrange(
                    "p (h v) -> p h v", v=33)
                C.vec(lambda e, o=vh[:, :, 0:32], i_=psh:
                      e.tensor_copy(o, i_), (pk,), (("V", stile),))
        _load_w_chunks(C, "l1Wo", wreg, ("WF",), wstage)
        for qb in range(4):
            schunks = list(range(4 * qb + 4))
            nsc = len(schunks)
            for h in range(16):
                cf, base = h // 2, 64 * (h % 2)
                pv = C.PS[2 + h % 2]
                pvk = ("PS", 2 + h % 2)
                for idx, sc in enumerate(schunks):
                    sp = C.PS[idx % 2]
                    spk = ("PS", idx % 2)
                    C.mm(sp, KT[cf][base:base + 32, 128 * sc:128 * (sc + 1)],
                         QT[cf][base:base + 32, 512 * qb:512 * (qb + 1)],
                         (("KT", cf), ("QT", cf)), spk, start=True,
                         stop=True)
                    jl = sc - 4 * qb
                    if jl >= 0:
                        C.vec(lambda e, o=stmp[idx % 2], a=sp, m=masks[jl]:
                              e.tensor_tensor(o, a, m, ALU.add),
                              (spk, ("mask", jl)), (("stmp", idx % 2),))
                        C.activ(probsT[idx % 2], stmp[idx % 2], AF.Exp,
                                (("stmp", idx % 2),), ("probs", idx % 2),
                                scale=0.125)
                    else:
                        C.activ(probsT[idx % 2], sp, AF.Exp,
                                (spk,), ("probs", idx % 2), scale=0.125)
                    C.mm(pv[0:33, :],
                         V[:, 528 * sc + 33 * h:528 * sc + 33 * (h + 1)],
                         probsT[idx % 2],
                         (("V", sc), ("V", "all"), ("probs", idx % 2)), pvk,
                         start=(idx == 0), stop=(idx == nsc - 1))
                C.vec(lambda e, o=rrowf[0:1, :], i_=pv[32:33, :]:
                      e.reciprocal(o, i_), (pvk,), (("rrow",),))
                C.activ(rrow[0:1, :], rrowf[0:1, :], AF.Copy, (("rrow",),),
                        ("rrow",))
                pb = C.PS[4]
                C.mm(pb[0:32, :], ones[0:1, 0:32], rrow[0:1, :],
                     (("rrow",), ("ones",)), ("PS", 4), start=True,
                     stop=True)
                C.activ(rb[0:32, :], pb[0:32, :], AF.Copy, (("PS", 4),),
                        ("rb",))
                C.vec(lambda e, o=ccT[cf][base:base + 32, :],
                      a=pv[0:32, :], m=rb[0:32, :]:
                      e.tensor_tensor(o, a, m, ALU.mult),
                      (pvk, ("rb",)), (("ccT", cf),))
            for qt_i in range(4):
                for dc in range(2):
                    ps = C.PS[5]
                    pk = ("PS", 5)
                    for cf in range(8):
                        C.mm(ps, ccT[cf][:, 128 * qt_i:128 * (qt_i + 1)],
                             wreg[cf][:, 512 * dc:512 * (dc + 1)],
                             (("ccT", cf), (("WF",), cf)), pk,
                             start=(cf == 0), stop=(cf == 7))
                    C.activ(ostg[:, 512 * dc:512 * (dc + 1)], ps, AF.Copy,
                            (pk,), ("ostg",))
                ti = 16 * b + 4 * qb + qt_i
                C.st(drr[ti], ostg, (("ostg",),))


def stage_G(C):
    stage_f = [C.f32(2048) for _ in range(2)]
    stage_b = [C.bf(2048) for _ in range(2)]
    s12 = C.w("f1w12T").rearrange("e (c p) j -> (e c) p j", p=128)
    d12 = C.act("w12bf").rearrange("e (c p) j -> (e c) p j", p=128)
    for i in range(64):
        for half in range(2):
            r = half
            C.ld(stage_f[r], s12[i][:, 2048 * half:2048 * (half + 1)],
                 ("cvf", r))
            C.activ(stage_b[r], stage_f[r], AF.Copy, (("cvf", r),),
                    ("cvb", r))
            C.st(d12[i][:, 2048 * half:2048 * (half + 1)], stage_b[r],
                 (("cvb", r),), wkey=("w12bf",))
    s3 = C.w("f1w3T").rearrange("e (c p) d -> (e c) p d", p=128)
    d3 = C.act("w3bf").rearrange("e (c p) d -> (e c) p d", p=128)
    for i in range(128):
        r = i % 2
        C.ld(stage_f[r][:, 0:1024], s3[i], ("cvf", r))
        C.activ(stage_b[r][:, 0:1024], stage_f[r][:, 0:1024], AF.Copy,
                (("cvf", r),), ("cvb", r))
        C.st(d3[i], stage_b[r][:, 0:1024], (("cvb", r),), wkey=("w3bf",))
    gwr = C.w("f1gwT").rearrange("(c p) e -> c p e", p=128)
    gwB = C.bf(64)
    for c in range(8):
        sk = ("cvf", c % 2)
        C.ld(stage_f[c % 2][:, 0:8], gwr[c], sk)
        C.activ(gwB[:, 8 * c:8 * (c + 1)], stage_f[c % 2][:, 0:8], AF.Copy,
                (sk,), ("gwB", c))
    xnTs = [C.bf(512) for _ in range(8)]
    h2b = [C.bf(512) for _ in range(16)]
    hid = [C.bf(512) for _ in range(16)]
    w12c = [C.bf(1024) for _ in range(2)]
    w3c = [C.bf(1024) for _ in range(16)]
    yac = [C.f32(1024) for _ in range(4)]
    sgate = C.f32(48)
    probs = C.f32(32)
    silu_t = C.f32(512)
    fstg = [C.f32(1024) for _ in range(2)]
    xnTr = C.act("xn1T").rearrange("(c p) t -> c p t", p=128)
    x1r = C.act("x1tot").rearrange("(n p) d -> n p d", p=128)
    a3r = C.act("attn3").rearrange("(n p) d -> n p d", p=128)
    outr = C.act("out").rearrange("(n p) d -> n p d", p=128)
    w12br = C.act("w12bf").rearrange("e (c p) j -> e c p j", p=128)
    w3br = C.act("w3bf").rearrange("e (c p) d -> (e c) p d", p=128)
    for sl in range(NSLAB):
        for cd in range(8):
            C.ld(xnTs[cd], xnTr[cd, :, 512 * sl:512 * (sl + 1)],
                 ("xnTs", cd))
        for k in range(4):
            C.vec(lambda e, o=yac[k]: e.memset(o, 0.0), (), (("yac", k),))
            ps = C.PS[4]
            pk = ("PS", 4)
            for cd in range(8):
                C.mm(ps[:, 0:8], xnTs[cd][:, 128 * k:128 * (k + 1)],
                     gwB[:, 8 * cd:8 * (cd + 1)],
                     (("xnTs", cd), ("gwB", cd)), pk,
                     start=(cd == 0), stop=(cd == 7))
            sc8 = sgate[:, 12 * k:12 * k + 8]
            m1 = sgate[:, 12 * k + 8:12 * k + 9]
            m2 = sgate[:, 12 * k + 9:12 * k + 10]
            zz = sgate[:, 12 * k + 10:12 * k + 11]
            pr8 = probs[:, 8 * k:8 * k + 8]
            C.vec(lambda e, o=sc8, i_=ps[:, 0:8]: e.tensor_copy(o, i_),
                  (pk,), (("sc", k),))
            C.vec(lambda e, o=m1, i_=sc8: e.tensor_reduce(
                o, i_, AX.X, ALU.max), (("sc", k),), (("m1", k),))
            C.vec(lambda e, o=pr8, i_=sc8, s=m1:
                  e.tensor_scalar_sub(o, i_, s),
                  (("sc", k), ("m1", k)), (("d1", k),))
            C.activ(pr8, pr8, AF.Sign, (("d1", k),), ("d1", k))
            C.sadd(pr8, pr8, 1.0, (("d1", k),), ("d1", k))
            C.vec(lambda e, o=pr8, i0=pr8, s=-1.0e9, i1=sc8:
                  e.scalar_tensor_tensor(o, i0, s, i1, ALU.mult, ALU.add),
                  (("d1", k), ("sc", k)), (("msk", k),))
            C.vec(lambda e, o=m2, i_=pr8: e.tensor_reduce(
                o, i_, AX.X, ALU.max), (("msk", k),), (("m2", k),))
            C.vec(lambda e, o=pr8, i_=sc8, s=m2:
                  e.tensor_scalar_sub(o, i_, s),
                  (("sc", k), ("m2", k)), (("d2", k),))
            C.activ(pr8, pr8, AF.Sign, (("d2", k),), ("d2", k))
            C.activ(pr8, pr8, AF.Sign, (("d2", k),), ("d2", k), bias=1.0)
            C.vec(lambda e, o=sc8, i_=sc8, s=m1:
                  e.tensor_scalar_sub(o, i_, s),
                  (("sc", k), ("m1", k)), (("sc", k),))
            C.activ(sc8, sc8, AF.Exp, (("sc", k),), ("sc", k))
            C.vec(lambda e, o=pr8, a=pr8, b=sc8:
                  e.tensor_tensor(o, a, b, ALU.mult),
                  (("d2", k), ("sc", k)), (("u", k),))
            C.vec(lambda e, o=zz, i_=pr8: e.reduce_sum(o, i_, axis=AX.X),
                  (("u", k),), (("z", k),))
            C.vec(lambda e, o=zz, i_=zz: e.reciprocal(o, i_),
                  (("z", k),), (("z", k),))
            C.vec(lambda e, o=pr8, i_=pr8, s=zz:
                  e.tensor_scalar_mul(o, i_, s),
                  (("u", k), ("z", k)), (("pr", k),))
        for e_ in range(8):
            for jj in range(32):
                j = (16 + jj) if jj < 16 else (jj - 16)
                r = jj % 2
                C.ld(w12c[r].rearrange("p (c j) -> p c j", j=128),
                     w12br[e_, :, :, 128 * j:128 * (j + 1)].rearrange(
                         "c p j -> p c j"),
                     ("w12c", r), rkeys=(("w12bf",),))
                ps = C.PS[jj % 2]
                pk = ("PS", jj % 2)
                for c in range(8):
                    C.mm(ps, w12c[r][:, 128 * c:128 * (c + 1)], xnTs[c],
                         (("w12c", r), ("xnTs", c)), pk,
                         start=(c == 0), stop=(c == 7))
                if j >= 16:
                    C.activ(h2b[j - 16], ps, AF.Copy, (pk,), ("h2", j - 16))
                else:
                    C.activ(silu_t, ps, AF.Silu, (pk,), ("silu",))
                    C.vec(lambda e, o=hid[j], a=silu_t, b=h2b[j]:
                          e.tensor_tensor(o, a, b, ALU.mult),
                          (("silu",), ("h2", j)), (("hid", j),))
                    C.ld(w3c[j], w3br[16 * e_ + j], ("w3c", j),
                         rkeys=(("w3bf",),))
            for tt in range(4):
                for dc in range(2):
                    ps = C.PS[2 + dc]
                    pk = ("PS", 2 + dc)
                    for hc in range(16):
                        C.mm(ps, hid[hc][:, 128 * tt:128 * (tt + 1)],
                             w3c[hc][:, 512 * dc:512 * (dc + 1)],
                             (("hid", hc), ("w3c", hc)), pk,
                             start=(hc == 0), stop=(hc == 15))
                    C.vec(lambda e, o=yac[tt][:, 512 * dc:512 * (dc + 1)],
                          i0=ps, s=probs[:, 8 * tt + e_:8 * tt + e_ + 1],
                          i1=yac[tt][:, 512 * dc:512 * (dc + 1)]:
                          e.scalar_tensor_tensor(o, i0, s, i1, ALU.mult,
                                                 ALU.add),
                          (pk, ("pr", tt), ("yac", tt)), (("yac", tt),))
        for k in range(4):
            i = 4 * sl + k
            r = k % 2
            C.ld(fstg[r], x1r[i], ("fstg", r))
            C.vec(lambda e, o=yac[k], a=yac[k], b=fstg[r]:
                  e.tensor_tensor(o, a, b, ALU.add),
                  (("yac", k), ("fstg", r)), (("yac", k),))
            C.ld(fstg[r], a3r[i], ("fstg", r))
            C.vec(lambda e, o=yac[k], a=yac[k], b=fstg[r]:
                  e.tensor_tensor(o, a, b, ALU.add),
                  (("yac", k), ("fstg", r)), (("yac", k),))
            C.st(outr[i], yac[k], (("yac", k),))


STAGES = {"A": stage_A, "B": stage_B, "C": stage_C, "D": stage_D,
          "E": stage_E, "F": stage_F, "G": stage_G}
ORDER = "ABCDEFG"


def build_group(group):
    nc = bass.Bass("TRN2", target_bir_lowering=False)
    C = Ctx(nc, group)
    C.ld(C.cscal[:, :], C.w("cScal"), ("cscal",))
    C.P.barrier()
    for s in group:
        C.reset_arena()
        STAGES[s](C)
        C.P.barrier()
    C.P.replay(nc)
    C.close()
    return nc, C


def prep_host(inputs):
    f32 = np.float32
    h = {}
    h["l0Wq"] = np.ascontiguousarray(
        inputs["L0_attn_Wq"].transpose(1, 0, 2).reshape(D, D).astype(f32))
    h["l0Wk"] = np.ascontiguousarray(
        inputs["L0_attn_Wk"].transpose(1, 0, 2).reshape(D, D).astype(f32))
    h["l0Wv"] = np.ascontiguousarray(
        inputs["L0_attn_Wv"].transpose(1, 0, 2).reshape(D, D).astype(f32))
    h["l0WoT"] = np.ascontiguousarray(inputs["L0_attn_Wo"].T.astype(f32))
    for nm, src in (("f0Wq", "L0_ffn_Wq"), ("f0Wk", "L0_ffn_Wk"),
                    ("f0Wv", "L0_ffn_Wv"), ("f0Wo", "L0_ffn_Wo")):
        h[nm] = np.ascontiguousarray(np.asarray(inputs[src]).astype(f32))
    for nm, src in (("f0bq", "L0_ffn_bq"), ("f0bk", "L0_ffn_bk"),
                    ("f0bv", "L0_ffn_bv"), ("f0bo", "L0_ffn_bo"),
                    ("f0g1", "L0_ffn_ln1_g"), ("f0b1n", "L0_ffn_ln1_b"),
                    ("f0g2", "L0_ffn_ln2_g"), ("f0b2n", "L0_ffn_ln2_b")):
        h[nm] = np.ascontiguousarray(
            np.asarray(inputs[src]).reshape(1, D).astype(f32))
    h["f0rw"] = np.ascontiguousarray(
        np.asarray(inputs["L0_ffn_router_w"]).reshape(1, D).astype(f32))
    h["f0rb"] = np.ascontiguousarray(
        np.asarray(inputs["L0_ffn_router_b"]).reshape(1, 1).astype(f32))
    h["f0W1"] = np.ascontiguousarray(np.asarray(inputs["L0_ffn_W1"]).astype(f32))
    h["f0b1"] = np.ascontiguousarray(np.asarray(inputs["L0_ffn_b1"]).astype(f32))
    h["f0W2"] = np.ascontiguousarray(np.asarray(inputs["L0_ffn_W2"]).astype(f32))
    h["f0b2"] = np.ascontiguousarray(np.asarray(inputs["L0_ffn_b2"]).astype(f32))
    h["l1Wq"] = np.ascontiguousarray(np.asarray(inputs["L1_attn_Wq"]).astype(f32))
    h["l1Wkv"] = np.ascontiguousarray(
        np.asarray(inputs["L1_attn_Wkv"]).astype(f32))
    h["l1Wk"] = np.ascontiguousarray(np.asarray(inputs["L1_attn_Wk"]).astype(f32))
    h["l1Wv"] = np.ascontiguousarray(np.asarray(inputs["L1_attn_Wv"]).astype(f32))
    h["l1Wo"] = np.ascontiguousarray(np.asarray(inputs["L1_attn_Wo"]).astype(f32))
    h["l1qg"] = np.ascontiguousarray(
        np.tile(np.asarray(inputs["L1_attn_q_g"]).astype(f32), 2).reshape(128, 1))
    kgv = np.asarray(inputs["L1_attn_k_g"]).astype(f32)
    h["l1kg"] = np.ascontiguousarray(np.concatenate(
        [kgv, np.zeros(32, f32), kgv, np.zeros(32, f32)]).reshape(128, 1))
    h["f1w12T"] = np.ascontiguousarray(
        np.asarray(inputs["L1_ffn_w12"]).transpose(0, 2, 1).astype(f32))
    h["f1w3T"] = np.ascontiguousarray(
        np.asarray(inputs["L1_ffn_w3"]).transpose(0, 2, 1).astype(f32))
    h["f1gwT"] = np.ascontiguousarray(
        np.asarray(inputs["L1_ffn_gate_w"]).T.astype(f32))
    h["cIdent"] = np.eye(128, dtype=BF)
    msk = np.zeros((4, 128, 512), np.float32)
    p = np.arange(128)[:, None]
    c = np.arange(512)[None, :]
    for jl in range(4):
        msk[jl] = np.where(p <= c - 128 * jl, 0.0, NEGBIG)
    h["cMask"] = msk
    blk = np.zeros((128, 2), BF)
    blk[0:64, 0] = 1
    blk[64:128, 1] = 1
    h["cBlk64"] = blk
    h["cBlk64T"] = np.ascontiguousarray(blk.T)
    cs = np.zeros((1, 32), np.float32)
    cs[0, :len(CONST_VALS)] = np.array(CONST_VALS, np.float32)
    h["cScal"] = np.ascontiguousarray(np.tile(cs, (128, 1)))
    return h


def kernel(**inputs):
    host = prep_host(inputs)
    xs = np.ascontiguousarray(np.asarray(inputs["L0_attn_x"]).astype(np.float32))
    shards = [np.ascontiguousarray(xs[2 * i:2 * i + 2]) for i in range(NCORE)]
    scratch = {}
    groups = [ORDER] if MODE == "fused" else [s for s in ORDER]
    for g in groups:
        nc, C = build_group(g)
        in_maps = []
        for c in range(NCORE):
            m = {}
            for n in C.used_in:
                if n == "x":
                    m[n] = shards[c]
                elif n in host:
                    m[n] = host[n]
                else:
                    m[n] = scratch[n][c]
            in_maps.append(m)
        res = run_bass_kernel_spmd(nc, in_maps, core_ids=list(range(NCORE)))
        for n in C.outs:
            scratch[n] = [np.asarray(res.results[c][n]) for c in range(NCORE)]
    outs = scratch["out"]
    return np.concatenate(
        [o.reshape(B_LOC, S, D) for o in outs], axis=0).astype(np.float32)


# version 02 (final)
# speedup vs baseline: 1.0177x; 1.0177x over previous
import contextlib
import numpy as np
import ml_dtypes
import concourse.bass as bass
import concourse.mybir as mybir
from concourse.bass_utils import run_bass_kernel_spmd

F32 = mybir.dt.float32
BF16 = mybir.dt.bfloat16
I32 = mybir.dt.int32
AF = mybir.ActivationFunctionType
ALU = mybir.AluOpType
AX = mybir.AxisListType

MODE = "fused"

NCORE = 8
B_LOC = 2
S = 2048
D = 1024
T_LOC = B_LOC * S
NT = T_LOC // 128
NSLAB = T_LOC // 512
EPS_LN = 1e-5
EPS_RMS = 1e-6
NEGBIG = -1.0e30
BF = ml_dtypes.bfloat16

ENGINES = ("sync", "gpsimd", "tensor", "vector", "scalar")


class Prog:
    def __init__(self):
        self.recs = {e: [] for e in ENGINES}
        self.last_w = {}
        self.readers = {}

    def emit(self, eng, fn, reads=(), writes=(), inc=1):
        deps = []
        for b in reads:
            lw = self.last_w.get(b)
            if lw is not None:
                deps.append(lw)
        for b in writes:
            lw = self.last_w.get(b)
            if lw is not None:
                deps.append(lw)
            deps.extend(self.readers.get(b, ()))
        dd = {}
        for (de, di) in deps:
            if de == eng == "tensor":
                continue
            if dd.get(de, -1) < di:
                dd[de] = di
        idx = len(self.recs[eng])
        self.recs[eng].append({"fn": fn, "deps": dd, "inc": inc, "car": False})
        me = (eng, idx)
        for b in reads:
            self.readers.setdefault(b, []).append(me)
        for b in writes:
            self.last_w[b] = me
            self.readers[b] = []
        return me

    def barrier(self):
        lasts = {e: len(self.recs[e]) - 1 for e in ENGINES if self.recs[e]}
        for e in ENGINES:
            dd = {de: di for de, di in lasts.items() if de != e}
            self.recs[e].append(
                {"fn": lambda eng: eng.nop(), "deps": dd, "inc": 1,
                 "car": False})
        self.last_w = {}
        self.readers = {}

    R_DMA = {"sync": 16, "gpsimd": 8, "tensor": 2, "vector": 2, "scalar": 2}

    def replay(self, nc):
        for e in ENGINES:
            for r in self.recs[e]:
                for de, di in r["deps"].items():
                    self.recs[de][di]["car"] = True
        for e in ENGINES:
            for r in self.recs[e]:
                if r["inc"] == 16:
                    r["car"] = True
        cumc = {}
        cntd = {}
        dmak = {}
        for e in ENGINES:
            c = 0
            n = 0
            cs = []
            ds = []
            ks = []
            for r in self.recs[e]:
                kk = -1
                if r["car"]:
                    if r["inc"] == 16:
                        kk = n
                        n += 1
                    else:
                        c += 1
                cs.append(c)
                ds.append(n)
                ks.append(kk)
            cumc[e] = cs
            cntd[e] = ds
            dmak[e] = ks

        def dma_waits(e, n):
            R = self.R_DMA[e]
            out = []
            for r_ in range(min(R, n)):
                c_r = n // R + (1 if r_ < n % R else 0)
                out.append((r_, 16 * c_r))
            return out

        with contextlib.ExitStack() as stk:
            semc = {e: stk.enter_context(nc.semaphore(name="semc_" + e))
                    for e in ENGINES}
            semd = {e: [stk.enter_context(
                nc.semaphore(name="semd_%s_%d" % (e, i)))
                for i in range(self.R_DMA[e])] for e in ENGINES}
            block = stk.enter_context(nc.Block())

            def mk(e):
                def body(eng):
                    wc = {o: 0 for o in ENGINES}
                    wd = {o: {} for o in ENGINES}
                    R = self.R_DMA[e]
                    for r in self.recs[e]:
                        for de in sorted(r["deps"]):
                            di = r["deps"][de]
                            tc = cumc[de][di]
                            if tc > wc[de]:
                                eng.wait_ge(semc[de], tc)
                                wc[de] = tc
                            nd = cntd[de][di]
                            if nd > 0:
                                for r_, v in dma_waits(de, nd):
                                    if wd[de].get(r_, 0) < v:
                                        eng.wait_ge(semd[de][r_], v)
                                        wd[de][r_] = v
                        kk = dmak[e][self.recs[e].index(r)] if False else None
                        kk = r.get("_k")
                        if kk is not None and kk >= R:
                            tgt = 16 * (kk // R)
                            if wd[e].get(kk % R, 0) < tgt:
                                eng.wait_ge(semd[e][kk % R], tgt)
                                wd[e][kk % R] = tgt
                        inst = r["fn"](eng)
                        if r["car"]:
                            if r["inc"] == 16:
                                inst.then_inc(semd[e][kk % R], 16)
                            else:
                                inst.then_inc(semc[e], 1)
                    if e == "gpsimd":
                        ntot = cntd[e][-1] if self.recs[e] else 0
                        for r_, v in dma_waits(e, ntot):
                            if wd[e].get(r_, 0) < v:
                                eng.wait_ge(semd[e][r_], v)
                return body
            for e in ENGINES:
                for i, r in enumerate(self.recs[e]):
                    r["_k"] = dmak[e][i] if dmak[e][i] >= 0 else None
            for e in ENGINES:
                getattr(block, e)(mk(e))


SCRATCH = {
    "xn_f32": ([T_LOC, D], F32),
    "xnT":    ([D, T_LOC], BF16),
    "qt0":    ([B_LOC, D, S], BF16),
    "kt0":    ([B_LOC, D, S], BF16),
    "v0":     ([B_LOC, S, 1040], BF16),
    "qt2":    ([B_LOC, D, S], BF16),
    "kt2":    ([B_LOC, D, S], BF16),
    "v2":     ([B_LOC, S, 1040], BF16),
    "attn1":  ([T_LOC, D], F32),
    "attn2":  ([T_LOC, D], F32),
    "ffn1":   ([T_LOC, D], F32),
    "x1tot":  ([T_LOC, D], F32),
    "xn1T":   ([D, T_LOC], BF16),
    "attn3":  ([T_LOC, D], F32),
}
SCR_PRODUCER = {
    "xn_f32": "A", "xnT": "A",
    "qt0": "B", "kt0": "B", "v0": "B", "qt2": "B", "kt2": "B", "v2": "B",
    "attn1": "C", "attn2": "C",
    "ffn1": "D",
    "x1tot": "E", "xn1T": "E",
    "attn3": "F",
}
WSPEC = {
    "x":      ([B_LOC, S, D], F32),
    "l0Wq":   ([D, D], F32), "l0Wk": ([D, D], F32), "l0Wv": ([D, D], F32),
    "l0WoT":  ([D, D], F32),
    "f0Wq":   ([D, D], F32), "f0Wk": ([D, D], F32), "f0Wv": ([D, D], F32),
    "f0Wo":   ([D, D], F32),
    "f0bq":   ([1, D], F32), "f0bk": ([1, D], F32), "f0bv": ([1, D], F32),
    "f0bo":   ([1, D], F32),
    "f0rw":   ([1, D], F32), "f0rb": ([1, 1], F32),
    "f0W1":   ([4, D, 4096], F32), "f0b1": ([4, 4096], F32),
    "f0W2":   ([4, 4096, D], F32), "f0b2": ([4, D], F32),
    "f0g1":   ([1, D], F32), "f0b1n": ([1, D], F32),
    "f0g2":   ([1, D], F32), "f0b2n": ([1, D], F32),
    "l1Wq":   ([D, D], F32), "l1Wkv": ([D, 128], F32),
    "l1Wk":   ([128, 512], F32), "l1Wv": ([128, 512], F32),
    "l1Wo":   ([D, D], F32),
    "l1qg":   ([128, 1], F32), "l1kg": ([128, 1], F32),
    "f1w12T": ([8, D, 4096], F32),
    "f1w3T":  ([8, 2048, D], F32),
    "f1gwT":  ([D, 8], F32),
    "cIdent": ([128, 128], BF16),
    "cMask":  ([4, 128, 512], F32),
    "cBlk64": ([128, 2], BF16),
    "cBlk64T": ([2, 128], BF16),
    "cScal":  ([128, 32], F32),
}
CONST_VALS = [1e-5, 1e-6, 0.125, 1.0 / 1024, 1.0 / 64, 1.0 / 32, 1.0, -1.0,
              -2.0, -3.0, 0.0, -0.5, -0.375, 1024.0, 0.25, -4.0, 0.7, -1e9,
              0.5, 2.0]
INTRA = {"w1bf", "w2bf", "b1bf", "b2bf", "w12bf", "w3bf"}
INTRA_SPEC = {
    "w1bf": ([4, D, 4096], BF16), "w2bf": ([4, 4096, D], BF16),
    "b1bf": ([4, 4096], BF16), "b2bf": ([4, D], BF16),
    "w12bf": ([8, D, 4096], BF16), "w3bf": ([8, 2048, D], BF16),
}


class Ctx:
    def __init__(self, nc, group):
        self.nc = nc
        self.P = Prog()
        self.group = group
        self.used_in = {}
        self.outs = []
        self._dram = {}
        self.off_bf = 0
        self.off_f32 = 0
        self._cm = contextlib.ExitStack()
        self.BFA = self._cm.enter_context(
            nc.sbuf_tensor("bfa", [128, 66560], BF16))
        self.F32A = self._cm.enter_context(
            nc.sbuf_tensor("f32a", [128, 15360], F32))
        self.PS = [self._cm.enter_context(
            nc.psum_tensor("ps%d" % i, [128, 512], F32))[:, :]
            for i in range(6)]
        self.PT = [self._cm.enter_context(
            nc.psum_tensor("pt%d" % i, [128, 128], BF16))[:, :]
            for i in range(2)]
        self.cscal = self._cm.enter_context(
            nc.sbuf_tensor("cscal", [128, 32], F32))
        for i, v in enumerate(CONST_VALS):
            nc.const_aps.register(F32, v, self.cscal[:, i:i + 1])

    def close(self):
        self._cm.close()

    def reset_arena(self):
        self.off_bf = 0
        self.off_f32 = 0

    def bf(self, cols):
        o = self.off_bf
        self.off_bf += (cols + 1) // 2 * 2
        assert self.off_bf <= 66560, "bf arena overflow"
        return self.BFA[:, o:o + cols]

    def f32(self, cols):
        o = self.off_f32
        self.off_f32 += cols
        assert self.off_f32 <= 15360, "f32 arena overflow"
        return self.F32A[:, o:o + cols]

    def w(self, name):
        if name not in self._dram:
            shape, dt = WSPEC[name]
            self._dram[name] = self.nc.dram_tensor(
                name, shape, dt, kind="ExternalInput").ap()
            self.used_in[name] = True
        return self._dram[name]

    def act(self, name):
        if name not in self._dram:
            if name == "out":
                shape, dt, kind = [T_LOC, D], F32, "ExternalOutput"
                self.outs.append(name)
            elif name in INTRA:
                shape, dt = INTRA_SPEC[name]
                kind = "Internal"
            else:
                shape, dt = SCRATCH[name]
                if len(self.group) > 1:
                    kind = "Internal"
                elif SCR_PRODUCER[name] in self.group:
                    kind = "ExternalOutput"
                    self.outs.append(name)
                else:
                    kind = "ExternalInput"
                    self.used_in[name] = True
            self._dram[name] = self.nc.dram_tensor(
                name, shape, dt, kind=kind).ap()
        return self._dram[name]

    def ld(self, dst, src, wkey, rkeys=()):
        self.P.emit("sync", lambda e, d=dst, s=src: e.dma_start(out=d, in_=s),
                    reads=tuple(rkeys), writes=(wkey,), inc=16)

    def st(self, dst, src, rkeys, wkey=None):
        wk = (wkey,) if wkey is not None else ()
        self.P.emit("gpsimd",
                    lambda e, d=dst, s=src: e.dma_start(out=d, in_=s),
                    reads=tuple(rkeys), writes=wk, inc=16)

    def mm(self, out, lhsT, rhs, rkeys, wkey, start, stop):
        self.P.emit("tensor",
                    lambda e, o=out, l=lhsT, r=rhs, s0=start, s1=stop:
                    e.matmul(o, l, r, start=s0, stop=s1),
                    reads=tuple(rkeys), writes=(wkey,))

    def tr(self, out, in_, ident, rkeys, wkey):
        self.P.emit("tensor",
                    lambda e, o=out, i=in_, d=ident: e.transpose(o, i, d),
                    reads=tuple(rkeys) + (("ident",),), writes=(wkey,))

    def activ(self, out, in_, func, rkeys, wkey, bias=None, scale=None):
        kw = {}
        if bias is not None:
            kw["bias"] = bias
        if scale is not None:
            kw["scale"] = scale
        self.P.emit("scalar",
                    lambda e, o=out, i=in_, f=func, k=kw:
                    e.activation(o, i, f, **k),
                    reads=tuple(rkeys), writes=(wkey,))

    def smul(self, out, in_, c, rkeys, wkey):
        self.P.emit("scalar", lambda e, o=out, i=in_, m=c: e.mul(o, i, m),
                    reads=tuple(rkeys), writes=(wkey,))

    def sadd(self, out, in_, c, rkeys, wkey):
        self.P.emit("scalar", lambda e, o=out, i=in_, a=c: e.add(o, i, a),
                    reads=tuple(rkeys), writes=(wkey,))

    def vec(self, fn, rkeys, wkeys):
        self.P.emit("vector", fn, reads=tuple(rkeys), writes=tuple(wkeys))

    def pool(self, fn, rkeys, wkeys):
        self.P.emit("gpsimd", fn, reads=tuple(rkeys), writes=tuple(wkeys))


def emit_ln_stats(C, xin, xkeys, xc, stats, tag, n=D, eps=EPS_LN,
                  xckey=None):
    if xckey is None:
        xckey = (tag, "xc")
    ssum = stats[:, 0:1]
    nmean = stats[:, 1:2]
    vsum = stats[:, 2:3]
    stdv = stats[:, 4:5]
    rstd = stats[:, 3:4]
    sk = (tag, "stats")
    C.vec(lambda e, o=ssum, i_=xin: e.reduce_sum(o, i_, axis=AX.X,
                                                 negate=True),
          xkeys, (sk,))
    C.smul(nmean, ssum, 1.0 / n, (sk,), sk)
    C.activ(xc, xin, AF.Identity, tuple(xkeys) + (sk,), xckey,
            bias=nmean)
    sq = C._sq
    C.activ(sq, xc, AF.Square, (xckey,), (tag, "sq"))
    C.vec(lambda e, o=vsum, i_=sq: e.reduce_sum(o, i_, axis=AX.X),
          ((tag, "sq"),), (sk,))
    C.activ(stdv, vsum, AF.Sqrt, (sk,), (sk,), bias=float(eps),
            scale=1.0 / n)
    C.vec(lambda e, o=rstd, i_=stdv: e.reciprocal(o, i_), (sk,), (sk,))
    return rstd, sk


def _ln_transpose_stage(C, src_is_sum):
    xr = C.w("x").rearrange("b (n p) d -> (b n) p d", p=128)
    if src_is_sum:
        a1r = C.act("attn1").rearrange("(n p) d -> n p d", p=128)
        f1r = C.act("ffn1").rearrange("(n p) d -> n p d", p=128)
        x1r = C.act("x1tot").rearrange("(n p) d -> n p d", p=128)
        xnTr = C.act("xn1T").rearrange("(c p) t -> c p t", p=128)
        xnr = None
    else:
        xnr = C.act("xn_f32").rearrange("(n p) d -> n p d", p=128)
        xnTr = C.act("xnT").rearrange("(c p) t -> c p t", p=128)
    ident = C.bf(128)
    C.ld(ident, C.w("cIdent"), ("ident",))
    xa = [C.f32(1024) for _ in range(2)]
    xb = [C.f32(1024) for _ in range(2)]
    xc = [C.f32(1024) for _ in range(2)]
    C._sq = C.f32(1024)
    stats = [C.f32(8) for _ in range(2)]
    xnb = [C.bf(1024) for _ in range(2)]
    xnf = [C.f32(1024) for _ in range(2)] if not src_is_sum else None
    slabT = [C.bf(512) for _ in range(8)]
    for sl in range(NSLAB):
        for k in range(4):
            i = 4 * sl + k
            r = i % 2
            C.ld(xa[r], xr[i], ("xa", r))
            if src_is_sum:
                C.ld(xb[r], a1r[i], ("xb", r))
                C.vec(lambda e, o=xa[r], a=xa[r], b=xb[r]:
                      e.tensor_tensor(o, a, b, ALU.add),
                      (("xa", r), ("xb", r)), (("xa", r),))
                C.ld(xb[r], f1r[i], ("xb", r))
                C.vec(lambda e, o=xa[r], a=xa[r], b=xb[r]:
                      e.tensor_tensor(o, a, b, ALU.add),
                      (("xa", r), ("xb", r)), (("xa", r),))
                C.st(x1r[i], xa[r], (("xa", r),))
            rstd, sk = emit_ln_stats(C, xa[r], (("xa", r),), xc[r],
                                     stats[r], ("ln", r))
            if src_is_sum:
                C.vec(lambda e, o=xnb[r], i_=xc[r], s=rstd:
                      e.tensor_scalar_mul(o, i_, s),
                      (("ln", r, "xc"), sk), (("xnb", r),))
            else:
                C.vec(lambda e, o=xnf[r], i_=xc[r], s=rstd:
                      e.tensor_scalar_mul(o, i_, s),
                      (("ln", r, "xc"), sk), (("xnf", r),))
                C.st(xnr[i], xnf[r], (("xnf", r),))
                C.activ(xnb[r], xnf[r], AF.Copy, (("xnf", r),), ("xnb", r))
            for c in range(8):
                pt = C.PT[c % 2]
                ptk = ("PT", c % 2)
                C.tr(pt, xnb[r][:, 128 * c:128 * (c + 1)], ident,
                     (("xnb", r),), ptk)
                C.vec(lambda e, o=slabT[c][:, 128 * k:128 * (k + 1)], i_=pt:
                      e.tensor_copy(o, i_), (ptk,), (("slabT", c),))
        for c in range(8):
            C.st(xnTr[c, :, 512 * sl:512 * (sl + 1)], slabT[c],
                 (("slabT", c),))


def stage_A(C):
    _ln_transpose_stage(C, False)


def stage_E(C):
    _ln_transpose_stage(C, True)


def _load_w_chunks(C, wname, wreg, tag, wstage, nchunk=8):
    wr = C.w(wname).rearrange("(c p) n -> c p n", p=128)
    for c in range(nchunk):
        sk = ("wstg", c % 2)
        C.ld(wstage[c % 2], wr[c], sk)
        C.activ(wreg[c], wstage[c % 2], AF.Copy, (sk,), (tag, c))


def stage_B(C):
    xnTr = C.act("xnT").rearrange("(c p) t -> c p t", p=128)
    xnT = [C.bf(4096) for _ in range(8)]
    for c in range(8):
        C.ld(xnT[c], xnTr[c], ("xnT", c))
    wstage = [C.f32(1024) for _ in range(2)]
    wreg = [[C.bf(1024) for _ in range(8)] for _ in range(2)]
    ev = [C.bf(512) for _ in range(2)]
    rows_f = C.f32(1024)
    rows = C.bf(1024)
    ones = C.bf(512)
    C.vec(lambda e, o=ones[0:1, :]: e.memset(o, 1.0), (), (("ones",),))
    v65 = C.bf(1040)

    def ld_bias(bname):
        C.ld(rows_f[0:1, :], C.w(bname), ("rowf",))
        C.activ(rows[0:1, :], rows_f[0:1, :], AF.Copy, (("rowf",),),
                ("row",))

    def fm_proj(wname, bname, dst, ws):
        tag = ("W", ws)
        _load_w_chunks(C, wname, wreg[ws], tag, wstage)
        if bname:
            ld_bias(bname)
        dr = C.act(dst).rearrange("b (c p) s -> b c p s", p=128)
        for b in range(B_LOC):
            for cf in range(8):
                for sl in range(4):
                    ps = C.PS[sl % 2]
                    pk = ("PS", sl % 2)
                    for cd in range(8):
                        C.mm(ps, wreg[ws][cd][:, 128 * cf:128 * (cf + 1)],
                             xnT[cd][:, 2048 * b + 512 * sl:
                                     2048 * b + 512 * (sl + 1)],
                             ((tag, cd), ("xnT", cd)), pk,
                             start=(cd == 0), stop=(cd == 7 and not bname))
                    if bname:
                        C.mm(ps, rows[0:1, 128 * cf:128 * (cf + 1)],
                             ones[0:1, :], (("row",), ("ones",)), pk,
                             start=False, stop=True)
                    C.activ(ev[sl % 2], ps, AF.Copy, (pk,), ("ev", sl % 2))
                    C.st(dr[b, cf, :, 512 * sl:512 * (sl + 1)], ev[sl % 2],
                         (("ev", sl % 2),))

    def tm_proj(wname, bname, dst, ws):
        tag = ("W", ws)
        _load_w_chunks(C, wname, wreg[ws], tag, wstage)
        if bname:
            ld_bias(bname)
        dr = C.act(dst).rearrange("b (n p) c -> b n p c", p=128)
        v65h = v65.rearrange("p (h v) -> p h v", v=65)
        for b in range(B_LOC):
            for stile in range(16):
                for hf in range(2):
                    ps = C.PS[hf]
                    pk = ("PS", hf)
                    for cd in range(8):
                        C.mm(ps, xnT[cd][:, 2048 * b + 128 * stile:
                                         2048 * b + 128 * (stile + 1)],
                             wreg[ws][cd][:, 512 * hf:512 * (hf + 1)],
                             ((tag, cd), ("xnT", cd)), pk,
                             start=(cd == 0), stop=(cd == 7 and not bname))
                    if bname:
                        C.mm(ps, ones[0:1, 0:128],
                             rows[0:1, 512 * hf:512 * (hf + 1)],
                             (("row",), ("ones",)), pk, start=False,
                             stop=True)
                    psh = ps.rearrange("p (h v) -> p h v", v=64)
                    C.vec(lambda e, o=v65h[:, 8 * hf:8 * (hf + 1), 0:64],
                          i_=psh: e.tensor_copy(o, i_),
                          (pk,), (("v65", hf),))
                C.vec(lambda e, o=v65h[:, :, 64:65]: e.memset(o, 1.0),
                      (), (("v65", "ones"),))
                C.st(dr[b, stile], v65,
                     (("v65", 0), ("v65", 1), ("v65", "ones")))

    fm_proj("l0Wq", None, "qt0", 0)
    fm_proj("l0Wk", None, "kt0", 1)
    tm_proj("l0Wv", None, "v0", 0)
    fm_proj("f0Wq", "f0bq", "qt2", 1)
    fm_proj("f0Wk", "f0bk", "kt2", 0)
    tm_proj("f0Wv", "f0bv", "v2", 1)


def _attention(C, qt, kt, vt, wo_name, bo_name, dst, causal,
               masks, wreg, wstage, rows_f, rows, ones, QT, KT, V, ccT,
               probsT, stmp, rrow, rrowf, rb, ostg):
    tag = ("WF",)
    _load_w_chunks(C, wo_name, wreg, tag, wstage)
    if bo_name:
        C.ld(rows_f[0:1, :], C.w(bo_name), ("rowf",))
        C.activ(rows[0:1, :], rows_f[0:1, :], AF.Copy, (("rowf",),),
                ("row",))
    qr = C.act(qt).rearrange("b (c p) s -> b c p s", p=128)
    kr = C.act(kt).rearrange("b (c p) s -> b c p s", p=128)
    vr = C.act(vt).rearrange("b (n p) c -> b n p c", p=128)
    drr = C.act(dst).rearrange("(n p) d -> n p d", p=128)
    for b in range(B_LOC):
        for c in range(8):
            C.ld(QT[c], qr[b, c], ("QT", c))
            C.ld(KT[c], kr[b, c], ("KT", c))
        for stile in range(16):
            C.ld(V[:, 1040 * stile:1040 * (stile + 1)], vr[b, stile],
                 ("V",))
        for qb in range(4):
            schunks = list(range(4 * qb + 4)) if causal else list(range(16))
            nsc = len(schunks)
            for h in range(16):
                cf, base = h // 2, 64 * (h % 2)
                pv = C.PS[2 + h % 2]
                pvk = ("PS", 2 + h % 2)
                for idx, sc in enumerate(schunks):
                    sp = C.PS[idx % 2]
                    spk = ("PS", idx % 2)
                    C.mm(sp, KT[cf][base:base + 64, 128 * sc:128 * (sc + 1)],
                         QT[cf][base:base + 64, 512 * qb:512 * (qb + 1)],
                         (("KT", cf), ("QT", cf)), spk, start=True,
                         stop=True)
                    jl = sc - 4 * qb
                    if causal and jl >= 0:
                        C.vec(lambda e, o=stmp[idx % 2], a=sp, m=masks[jl]:
                              e.tensor_tensor(o, a, m, ALU.add),
                              (spk, ("mask", jl)), (("stmp", idx % 2),))
                        C.activ(probsT[idx % 2], stmp[idx % 2], AF.Exp,
                                (("stmp", idx % 2),), ("probs", idx % 2),
                                scale=0.125)
                    else:
                        C.activ(probsT[idx % 2], sp, AF.Exp,
                                (spk,), ("probs", idx % 2), scale=0.125)
                    C.mm(pv[0:65, :],
                         V[:, 1040 * sc + 65 * h:1040 * sc + 65 * (h + 1)],
                         probsT[idx % 2],
                         (("V",), ("probs", idx % 2)), pvk,
                         start=(idx == 0), stop=(idx == nsc - 1))
                C.vec(lambda e, o=rrowf[0:1, :], i_=pv[64:65, :]:
                      e.reciprocal(o, i_), (pvk,), (("rrow",),))
                C.activ(rrow[0:1, :], rrowf[0:1, :], AF.Copy, (("rrow",),),
                        ("rrow",))
                pb = C.PS[4]
                C.mm(pb[0:64, :], ones[0:1, 0:64], rrow[0:1, :],
                     (("rrow",), ("ones",)), ("PS", 4), start=True,
                     stop=True)
                C.activ(rb[0:64, :], pb[0:64, :], AF.Copy, (("PS", 4),),
                        ("rb",))
                C.vec(lambda e, o=ccT[cf][base:base + 64, :],
                      a=pv[0:64, :], m=rb[0:64, :]:
                      e.tensor_tensor(o, a, m, ALU.mult),
                      (pvk, ("rb",)), (("ccT", cf),))
            for qt_i in range(4):
                for dc in range(2):
                    ps = C.PS[5]
                    pk = ("PS", 5)
                    for cf in range(8):
                        C.mm(ps, ccT[cf][:, 128 * qt_i:128 * (qt_i + 1)],
                             wreg[cf][:, 512 * dc:512 * (dc + 1)],
                             (("ccT", cf), (tag, cf)), pk,
                             start=(cf == 0),
                             stop=(cf == 7 and not bo_name))
                    if bo_name:
                        C.mm(ps, ones[0:1, 0:128],
                             rows[0:1, 512 * dc:512 * (dc + 1)],
                             (("row",), ("ones",)), pk, start=False,
                             stop=True)
                    C.activ(ostg[:, 512 * dc:512 * (dc + 1)], ps, AF.Copy,
                            (pk,), ("ostg",))
                ti = 16 * b + 4 * qb + qt_i
                C.st(drr[ti], ostg, (("ostg",),))


def stage_C(C):
    masks = [C.f32(512) for _ in range(4)]
    mr = C.w("cMask")
    for jl in range(4):
        C.ld(masks[jl], mr[jl], ("mask", jl))
    wstage = [C.f32(1024) for _ in range(2)]
    wreg = [C.bf(1024) for _ in range(8)]
    rows_f = C.f32(1024)
    rows = C.bf(1024)
    ones = C.bf(512)
    C.vec(lambda e, o=ones[0:1, :]: e.memset(o, 1.0), (), (("ones",),))
    QT = [C.bf(2048) for _ in range(8)]
    KT = [C.bf(2048) for _ in range(8)]
    V = C.bf(16640)
    ccT = [C.bf(512) for _ in range(8)]
    probsT = [C.bf(512) for _ in range(2)]
    stmp = [C.f32(512) for _ in range(2)]
    rrow = C.bf(512)
    rrowf = C.f32(512)
    rb = C.bf(512)
    ostg = C.f32(1024)
    _attention(C, "qt0", "kt0", "v0", "l0WoT", None, "attn1", True,
               masks, wreg, wstage, rows_f, rows, ones, QT, KT, V, ccT,
               probsT, stmp, rrow, rrowf, rb, ostg)
    _attention(C, "qt2", "kt2", "v2", "f0Wo", "f0bo", "attn2", False,
               masks, wreg, wstage, rows_f, rows, ones, QT, KT, V, ccT,
               probsT, stmp, rrow, rrowf, rb, ostg)


def stage_D(C):
    x1f = [C.f32(1024) for _ in range(4)]
    stage_f = [x1f[0].tensor[:, x1f[0].offsets[1]:x1f[0].offsets[1] + 2048]
               if False else None for _ in range(2)]
    stage_f = [C.F32A[:, 0:2048], C.F32A[:, 2048:4096]]
    stage_b = [C.bf(2048) for _ in range(2)]
    w1s = C.w("f0W1").rearrange("e (c p) j -> (e c) p j", p=128)
    w1d = C.act("w1bf").rearrange("e (c p) j -> (e c) p j", p=128)
    for i in range(32):
        for half in range(2):
            r = half
            C.ld(stage_f[r], w1s[i][:, 2048 * half:2048 * (half + 1)],
                 ("cvf", r))
            C.activ(stage_b[r], stage_f[r], AF.Copy, (("cvf", r),),
                    ("cvb", r))
            C.st(w1d[i][:, 2048 * half:2048 * (half + 1)], stage_b[r],
                 (("cvb", r),), wkey=("w1bf",))
    w2s = C.w("f0W2").rearrange("e (c p) d -> (e c) p d", p=128)
    w2d = C.act("w2bf").rearrange("e (c p) d -> (e c) p d", p=128)
    for i in range(128):
        r = i % 2
        C.ld(stage_f[r][:, 0:1024], w2s[i], ("cvf", r))
        C.activ(stage_b[r][:, 0:1024], stage_f[r][:, 0:1024], AF.Copy,
                (("cvf", r),), ("cvb", r))
        C.st(w2d[i], stage_b[r][:, 0:1024], (("cvb", r),), wkey=("w2bf",))
    b1s = C.w("f0b1")
    b1d = C.act("b1bf")
    for half in range(2):
        C.ld(stage_f[half][0:4, :], b1s[:, 2048 * half:2048 * (half + 1)],
             ("cvf", half))
        C.activ(stage_b[half][0:4, :], stage_f[half][0:4, :], AF.Copy,
                (("cvf", half),), ("cvb", half))
        C.st(b1d[:, 2048 * half:2048 * (half + 1)], stage_b[half][0:4, :],
             (("cvb", half),), wkey=("b1bf",))
    C.ld(stage_f[0][0:4, 0:1024], C.w("f0b2"), ("cvf", 0))
    C.activ(stage_b[0][0:4, 0:1024], stage_f[0][0:4, 0:1024], AF.Copy,
            (("cvf", 0),), ("cvb", 0))
    C.st(C.act("b2bf"), stage_b[0][0:4, 0:1024], (("cvb", 0),),
         wkey=("b2bf",))
    C.off_f32 = 4096
    rowf = C.f32(1024)
    rwbc = C.f32(1024)
    rbbc = C.f32(1)
    g1bc = C.f32(1024)
    b1nbc = C.f32(1024)
    g2bc = C.bf(1024)
    b2nbc = C.bf(1024)
    onesf = C.f32(128)
    C.vec(lambda e, o=onesf[0:1, :]: e.memset(o, 1.0), (), (("onesf",),))

    def bcast_row(name, src_cols, dstt, key):
        ps = C.PS[4]
        nn = src_cols
        for c0 in range(0, nn, 512):
            w = min(512, nn - c0)
            C.mm(ps[:, 0:w], onesf[0:1, :], rowf[0:1, c0:c0 + w],
                 (("onesf",), ("rowf",)), ("PS", 4), start=True, stop=True)
            C.activ(dstt[:, c0:c0 + w], ps[:, 0:w], AF.Copy, (("PS", 4),),
                    key)

    for name, dstt in (("f0g1", g1bc), ("f0b1n", b1nbc), ("f0g2", g2bc),
                       ("f0b2n", b2nbc)):
        C.ld(rowf[0:1, :], C.w(name), ("rowf",))
        bcast_row(name, 1024, dstt, (name, "bc"))
    C.ld(rowf[0:1, :], C.w("f0rw"), ("rowf",))
    bcast_row("f0rw", 1024, rwbc, ("rw", "bc"))
    C.ld(rowf[0:1, 0:1], C.w("f0rb"), ("rowf",))
    bcast_row("f0rb", 1, rbbc, ("rb", "bc"))
    ident = C.bf(128)
    C.ld(ident, C.w("cIdent"), ("ident",))
    ones = C.bf(512)
    C.vec(lambda e, o=ones[0:1, :]: e.memset(o, 1.0), (), (("ones",),))
    det = C.f32(8)
    deti = C._cm.enter_context(C.nc.sbuf_tensor("deti", [128, 4], I32))
    cflag = det[:, 0:1]
    ncflag = det[:, 1:2]
    C.vec(lambda e, o=det[:, 2:3]: e.memset(o, 0.7), (), (("det",),))
    C.vec(lambda e, o=deti[:, 0:1], i_=det[:, 2:3]: e.tensor_copy(o, i_),
          (("det",),), (("det",),))
    C.vec(lambda e, o=cflag, i_=deti[:, 0:1]: e.tensor_copy(o, i_),
          (("det",),), (("det",),))
    C.vec(lambda e, o=ncflag, i_=cflag:
          e.tensor_scalar(o, i_, -1.0, -1.0, ALU.mult, ALU.subtract),
          (("det",),), (("det",),))
    ffs = [C.f32(1024) for _ in range(4)]
    C._sq = C.f32(1024)
    stats = C.f32(8)
    keysb = C.f32(16)
    inds = C.f32(16)
    scr2 = C.f32(2)
    scrD = C.f32(1024)
    xcb = scrD
    o2t = scrD
    a2_tile = scrD
    xnb_tile = rowf
    x1b = [C.bf(1024) for _ in range(4)]
    x1T = [C.bf(512) for _ in range(8)]
    hT = [C.bf(512) for _ in range(32)]
    w1c = [C.bf(1024) for _ in range(2)]
    w2c = [C.bf(512) for _ in range(32)]
    b1row = C.bf(4096)
    b2row = C.bf(1024)
    xnr = C.act("xn_f32").rearrange("(n p) d -> n p d", p=128)
    a2r = C.act("attn2").rearrange("(n p) d -> n p d", p=128)
    ffr = C.act("ffn1").rearrange("(n p) d -> n p d", p=128)
    w1br = C.act("w1bf").rearrange("e (c p) j -> e c p j", p=128)
    w2br = C.act("w2bf").rearrange("e (c p) d -> (e c) p d", p=128)

    def f2i_floor(src, skey, delta_rn, colA, colB, out_col, okey):
        C.sadd(colA, src, delta_rn, (skey,), ("kf", "a"))
        C.vec(lambda e, o=deti[:, 1:2], i_=colA: e.tensor_copy(o, i_),
              (("kf", "a"),), (("kf", "ia"),))
        C.vec(lambda e, o=colA, i_=deti[:, 1:2]: e.tensor_copy(o, i_),
              (("kf", "ia"),), (("kf", "a"),))
        C.vec(lambda e, o=deti[:, 2:3], i_=src: e.tensor_copy(o, i_),
              (skey,), (("kf", "ib"),))
        C.vec(lambda e, o=colB, i_=deti[:, 2:3]: e.tensor_copy(o, i_),
              (("kf", "ib"),), (("kf", "b"),))
        C.vec(lambda e, o=colA, i_=colA, s=cflag:
              e.tensor_scalar_mul(o, i_, s),
              (("kf", "a"), ("det",)), (("kf", "a"),))
        C.vec(lambda e, o=out_col, i0=colB, s=ncflag, i1=colA:
              e.scalar_tensor_tensor(o, i0, s, i1, ALU.mult, ALU.add),
              (("kf", "a"), ("kf", "b"), ("det",)), (okey,))

    for sl in range(NSLAB):
        for k in range(4):
            C.vec(lambda e, o=ffs[k]: e.memset(o, 0.0), (), (("ff", k),))
        for k in range(4):
            i = 4 * sl + k
            C.ld(xnb_tile, xnr[i], ("rowf",))
            C.ld(a2_tile, a2r[i], ("a2T",))
            extra = (("cvb", 0), ("cvb", 1)) if sl == 0 else ()
            C.P.emit("vector",
                     lambda e, o=x1f[k], a=xnb_tile, b=a2_tile:
                     e.tensor_tensor(o, a, b, ALU.add),
                     reads=(("rowf",), ("a2T",)) + extra,
                     writes=(("x1f", k),))
            rstd, sk = emit_ln_stats(C, x1f[k], (("x1f", k),), xcb, stats,
                                     ("lnD",), xckey=("a2T",))
            C.vec(lambda e, o=x1f[k], i0=xcb, s=rstd, i1=g1bc:
                  e.scalar_tensor_tensor(o, i0, s, i1, ALU.mult, ALU.mult),
                  (("a2T",), sk, ("f0g1", "bc")), (("x1f", k),))
            C.vec(lambda e, o=x1f[k], a=x1f[k], b=b1nbc:
                  e.tensor_tensor(o, a, b, ALU.add),
                  (("x1f", k), ("f0b1n", "bc")), (("x1f", k),))
            C.activ(x1b[k], x1f[k], AF.Copy, (("x1f", k),), ("x1b", k))
            C.vec(lambda e, o=C._sq, a=x1f[k], b=rwbc:
                  e.tensor_tensor(o, a, b, ALU.mult),
                  (("x1f", k), ("rw", "bc")), (("lnD", "sq"),))
            kcol = keysb[:, 4 * k:4 * k + 1]
            gcol = keysb[:, 4 * k + 1:4 * k + 2]
            fcol = keysb[:, 4 * k + 2:4 * k + 3]
            qcol = keysb[:, 4 * k + 3:4 * k + 4]
            C.vec(lambda e, o=kcol, i_=C._sq: e.reduce_sum(o, i_, axis=AX.X),
                  (("lnD", "sq"),), (("keys", k),))
            C.vec(lambda e, o=kcol, a=kcol, b=rbbc:
                  e.tensor_tensor(o, a, b, ALU.add),
                  (("keys", k), ("rb", "bc")), (("keys", k),))
            C.sadd(gcol, kcol, 1024.0, (("keys", k),), ("keys", k))
            f2i_floor(gcol, ("keys", k), -0.5, scr2[:, 0:1], scr2[:, 1:2],
                      fcol, ("fl", k))
            C.smul(gcol, fcol, 0.25, (("fl", k),), ("keys", k))
            f2i_floor(gcol, ("keys", k), -0.375, scr2[:, 0:1], scr2[:, 1:2],
                      qcol, ("q", k))
            C.vec(lambda e, o=fcol, i0=qcol, s=-4.0, i1=fcol:
                  e.scalar_tensor_tensor(o, i0, s, i1, ALU.mult, ALU.add),
                  (("q", k), ("fl", k)), (("m", k),))
            for e_ in range(4):
                icol = inds[:, 4 * k + e_:4 * k + e_ + 1]
                C.activ(icol, fcol, AF.Square, (("m", k),), ("ind", k, e_),
                        bias=float(-e_))
                C.activ(icol, icol, AF.Relu, (("ind", k, e_),),
                        ("ind", k, e_), bias=1.0, scale=-1.0)
            for c in range(8):
                pt = C.PT[c % 2]
                ptk = ("PT", c % 2)
                C.tr(pt, x1b[k][:, 128 * c:128 * (c + 1)], ident,
                     (("x1b", k),), ptk)
                C.vec(lambda e, o=x1T[c][:, 128 * k:128 * (k + 1)], i_=pt:
                      e.tensor_copy(o, i_), (ptk,), (("x1T", c),))
        for e_ in range(4):
            C.ld(b1row[0:1, :], b1d[e_:e_ + 1, :], ("b1row",),
                 rkeys=(("b1bf",),))
            C.ld(b2row[0:1, :], C.act("b2bf")[e_:e_ + 1, :], ("b2row",),
                 rkeys=(("b2bf",),))
            for j in range(32):
                r = j % 2
                C.ld(w1c[r].rearrange("p (c j) -> p c j", j=128),
                     w1br[e_, :, :, 128 * j:128 * (j + 1)].rearrange(
                         "c p j -> p c j"),
                     ("w1c", r), rkeys=(("w1bf",),))
                ps = C.PS[j % 2]
                pk = ("PS", j % 2)
                for c in range(8):
                    C.mm(ps, w1c[r][:, 128 * c:128 * (c + 1)], x1T[c],
                         (("w1c", r), ("x1T", c)), pk,
                         start=(c == 0), stop=False)
                C.mm(ps, b1row[0:1, 128 * j:128 * (j + 1)], ones[0:1, :],
                     (("b1row",), ("ones",)), pk, start=False, stop=True)
                C.activ(hT[j], ps, AF.Relu, (pk,), ("hT", j))
            for dc in range(2):
                for j in range(32):
                    C.ld(w2c[j], w2br[32 * e_ + j][:, 512 * dc:512 * (dc + 1)],
                         ("w2c", j), rkeys=(("w2bf",),))
                for tt in range(4):
                    ps = C.PS[2 + tt % 2]
                    pk = ("PS", 2 + tt % 2)
                    for j in range(32):
                        C.mm(ps, hT[j][:, 128 * tt:128 * (tt + 1)], w2c[j],
                             (("hT", j), ("w2c", j)), pk,
                             start=(j == 0), stop=False)
                    C.mm(ps, ones[0:1, 0:128],
                         b2row[0:1, 512 * dc:512 * (dc + 1)],
                         (("b2row",), ("ones",)), pk, start=False, stop=True)
                    C.vec(lambda e, o=ffs[tt][:, 512 * dc:512 * (dc + 1)],
                          i0=ps, s=inds[:, 4 * tt + e_:4 * tt + e_ + 1],
                          i1=ffs[tt][:, 512 * dc:512 * (dc + 1)]:
                          e.scalar_tensor_tensor(o, i0, s, i1, ALU.mult,
                                                 ALU.add),
                          (pk, ("ind", tt, e_), ("ff", tt)), (("ff", tt),))
        for k in range(4):
            i = 4 * sl + k
            C.vec(lambda e, o=xcb, a=x1f[k], b=ffs[k]:
                  e.tensor_tensor(o, a, b, ALU.add),
                  (("x1f", k), ("ff", k)), (("a2T",),))
            rstd, sk = emit_ln_stats(C, xcb, (("a2T",),), x1f[k], stats,
                                     ("lnE",), xckey=("x1f", k))
            C.vec(lambda e, o=o2t, i0=x1f[k], s=rstd, i1=g2bc:
                  e.scalar_tensor_tensor(o, i0, s, i1, ALU.mult, ALU.mult),
                  (("x1f", k), sk, ("f0g2", "bc")), (("a2T",),))
            C.vec(lambda e, o=o2t, a=o2t, b=b2nbc:
                  e.tensor_tensor(o, a, b, ALU.add),
                  (("a2T",), ("f0b2n", "bc")), (("a2T",),))
            C.st(ffr[i], o2t, (("a2T",),))


def stage_F(C):
    masks = [C.f32(512) for _ in range(4)]
    mr = C.w("cMask")
    for jl in range(4):
        C.ld(masks[jl], mr[jl], ("mask", jl))
    wstage = [C.f32(1024) for _ in range(2)]
    wreg = [C.bf(1024) for _ in range(8)]
    wkv = C.bf(1024)
    wkb = C.bf(512)
    wvb = C.bf(512)
    ones = C.bf(512)
    C.vec(lambda e, o=ones[0:1, :]: e.memset(o, 1.0), (), (("ones",),))
    blk = C.bf(2)
    blkT = C.bf(128)
    C.ld(blk, C.w("cBlk64"), ("blk",))
    C.ld(blkT[0:2, :], C.w("cBlk64T"), ("blkT",))
    qg = C.f32(1)
    kg = C.f32(1)
    C.ld(qg, C.w("l1qg"), ("qg",))
    C.ld(kg, C.w("l1kg"), ("kg",))
    QT = [C.bf(2048) for _ in range(8)]
    KT = [C.bf(2048) for _ in range(8)]
    V = C.bf(8448)
    kvT = C.bf(2048)
    ccT = [C.bf(512) for _ in range(8)]
    probsT = [C.bf(512) for _ in range(2)]
    stmp = [C.f32(512) for _ in range(2)]
    rrow = C.bf(512)
    rb = C.bf(512)
    ostg = C.f32(1024)
    xnTs = [C.bf(512) for _ in range(8)]
    qtmp = C.f32(512)
    qsq = C.bf(512)
    sstd = C.f32(512)
    rstf = C.f32(512)
    rrowf = C.f32(512)
    rstd2 = C.bf(512)
    rbq = C.bf(512)
    xnTr = C.act("xn1T").rearrange("(c p) t -> c p t", p=128)
    drr = C.act("attn3").rearrange("(n p) d -> n p d", p=128)
    kvr = C.w("l1Wkv").rearrange("(c p) l -> c p l", p=128)

    def rms_norm_tile(dst_ap, dkey, src_f32, skeys, gain, gkey, nscale):
        C.activ(qsq, src_f32, AF.Square, skeys, ("qsq",))
        pb = C.PS[4]
        C.mm(pb[0:2, :], blk, qsq, (("blk",), ("qsq",)), ("PS", 4),
             start=True, stop=True)
        C.activ(sstd[0:2, :], pb[0:2, :], AF.Sqrt, (("PS", 4),), ("sstd",),
                bias=float(EPS_RMS), scale=nscale)
        C.vec(lambda e, o=rstf[0:2, :], i_=sstd[0:2, :]:
              e.reciprocal(o, i_), (("sstd",),), (("rstd2",),))
        C.activ(rstd2[0:2, :], rstf[0:2, :], AF.Copy, (("rstd2",),),
                ("rstd2",))
        C.mm(pb, blkT[0:2, :], rstd2[0:2, :],
             (("blkT",), ("rstd2",)), ("PS", 4), start=True, stop=True)
        C.activ(rbq, pb, AF.Copy, (("PS", 4),), ("rbq",))
        C.vec(lambda e, o=qsq, a=src_f32, b=rbq:
              e.tensor_tensor(o, a, b, ALU.mult),
              tuple(skeys) + (("rbq",),), (("qsq",),))
        C.vec(lambda e, o=dst_ap, i_=qsq, s=gain:
              e.tensor_scalar_mul(o, i_, s),
              (("qsq",), gkey), (dkey,))

    for b in range(B_LOC):
        C.vec(lambda e, o=V: e.memset(o, 1.0), (), (("V", "all"),))
        for c in range(8):
            C.vec(lambda e, o=KT[c]: e.memset(o, 0.0), (), (("KT", c),))
            C.vec(lambda e, o=ccT[c]: e.memset(o, 0.0), (), (("ccT", c),))
        _load_w_chunks(C, "l1Wq", wreg, ("WF",), wstage)
        for c in range(8):
            sk = ("wstg", c % 2)
            C.ld(wstage[c % 2][:, 0:128], kvr[c], sk)
            C.activ(wkv[:, 128 * c:128 * (c + 1)], wstage[c % 2][:, 0:128],
                    AF.Copy, (sk,), ("Wkv", c))
        C.ld(wstage[0][:, 0:512], C.w("l1Wk"), ("wstg", 0))
        C.activ(wkb, wstage[0][:, 0:512], AF.Copy, (("wstg", 0),), ("Wk",))
        C.ld(wstage[0][:, 0:512], C.w("l1Wv"), ("wstg", 0))
        C.activ(wvb, wstage[0][:, 0:512], AF.Copy, (("wstg", 0),), ("Wv",))
        for sl in range(4):
            for cd in range(8):
                C.ld(xnTs[cd],
                     xnTr[cd, :, 2048 * b + 512 * sl:
                          2048 * b + 512 * (sl + 1)],
                     ("xnTs", cd))
            for cf in range(8):
                ps = C.PS[cf % 2]
                pk = ("PS", cf % 2)
                for cd in range(8):
                    C.mm(ps, wreg[cd][:, 128 * cf:128 * (cf + 1)], xnTs[cd],
                         ((("WF",), cd), ("xnTs", cd)), pk,
                         start=(cd == 0), stop=(cd == 7))
                C.activ(qtmp, ps, AF.Copy, (pk,), ("qtmp",))
                rms_norm_tile(QT[cf][:, 512 * sl:512 * (sl + 1)],
                              ("QT", cf), qtmp, (("qtmp",),), qg, ("qg",),
                              1.0 / 64)
            ps = C.PS[2]
            pk = ("PS", 2)
            for cd in range(8):
                C.mm(ps, wkv[:, 128 * cd:128 * (cd + 1)], xnTs[cd],
                     (("Wkv", cd), ("xnTs", cd)), pk,
                     start=(cd == 0), stop=(cd == 7))
            C.activ(kvT[:, 512 * sl:512 * (sl + 1)], ps, AF.Copy, (pk,),
                    ("kvT",))
            for c in range(8):
                ps = C.PS[3]
                pk = ("PS", 3)
                for hh in range(2):
                    h = 2 * c + hh
                    C.mm(ps[64 * hh:64 * hh + 32, :],
                         wkb[:, 32 * h:32 * (h + 1)],
                         kvT[:, 512 * sl:512 * (sl + 1)],
                         (("Wk",), ("kvT",)), pk, start=True, stop=True)
                for hh in range(2):
                    C.vec(lambda e,
                          o=KT[c][64 * hh:64 * hh + 32,
                                  512 * sl:512 * (sl + 1)],
                          i_=ps[64 * hh:64 * hh + 32, :]:
                          e.tensor_copy(o, i_), (pk,), (("KT", c),))
                C.vec(lambda e, o=qtmp,
                      i_=KT[c][:, 512 * sl:512 * (sl + 1)]:
                      e.tensor_copy(o, i_), (("KT", c),), (("qtmp",),))
                rms_norm_tile(KT[c][:, 512 * sl:512 * (sl + 1)],
                              ("KT", c), qtmp, (("qtmp",),), kg, ("kg",),
                              1.0 / 32)
            for st_ in range(4):
                stile = 4 * sl + st_
                ps = C.PS[5]
                pk = ("PS", 5)
                C.mm(ps, kvT[:, 512 * sl + 128 * st_:
                             512 * sl + 128 * (st_ + 1)],
                     wvb, (("kvT",), ("Wv",)), pk, start=True, stop=True)
                psh = ps.rearrange("p (h v) -> p h v", v=32)
                vh = V[:, 528 * stile:528 * (stile + 1)].rearrange(
                    "p (h v) -> p h v", v=33)
                C.vec(lambda e, o=vh[:, :, 0:32], i_=psh:
                      e.tensor_copy(o, i_), (pk,), (("V", stile),))
        _load_w_chunks(C, "l1Wo", wreg, ("WF",), wstage)
        for qb in range(4):
            schunks = list(range(4 * qb + 4))
            nsc = len(schunks)
            for h in range(16):
                cf, base = h // 2, 64 * (h % 2)
                pv = C.PS[2 + h % 2]
                pvk = ("PS", 2 + h % 2)
                for idx, sc in enumerate(schunks):
                    sp = C.PS[idx % 2]
                    spk = ("PS", idx % 2)
                    C.mm(sp, KT[cf][base:base + 32, 128 * sc:128 * (sc + 1)],
                         QT[cf][base:base + 32, 512 * qb:512 * (qb + 1)],
                         (("KT", cf), ("QT", cf)), spk, start=True,
                         stop=True)
                    jl = sc - 4 * qb
                    if jl >= 0:
                        C.vec(lambda e, o=stmp[idx % 2], a=sp, m=masks[jl]:
                              e.tensor_tensor(o, a, m, ALU.add),
                              (spk, ("mask", jl)), (("stmp", idx % 2),))
                        C.activ(probsT[idx % 2], stmp[idx % 2], AF.Exp,
                                (("stmp", idx % 2),), ("probs", idx % 2),
                                scale=0.125)
                    else:
                        C.activ(probsT[idx % 2], sp, AF.Exp,
                                (spk,), ("probs", idx % 2), scale=0.125)
                    C.mm(pv[0:33, :],
                         V[:, 528 * sc + 33 * h:528 * sc + 33 * (h + 1)],
                         probsT[idx % 2],
                         (("V", sc), ("V", "all"), ("probs", idx % 2)), pvk,
                         start=(idx == 0), stop=(idx == nsc - 1))
                C.vec(lambda e, o=rrowf[0:1, :], i_=pv[32:33, :]:
                      e.reciprocal(o, i_), (pvk,), (("rrow",),))
                C.activ(rrow[0:1, :], rrowf[0:1, :], AF.Copy, (("rrow",),),
                        ("rrow",))
                pb = C.PS[4]
                C.mm(pb[0:32, :], ones[0:1, 0:32], rrow[0:1, :],
                     (("rrow",), ("ones",)), ("PS", 4), start=True,
                     stop=True)
                C.activ(rb[0:32, :], pb[0:32, :], AF.Copy, (("PS", 4),),
                        ("rb",))
                C.vec(lambda e, o=ccT[cf][base:base + 32, :],
                      a=pv[0:32, :], m=rb[0:32, :]:
                      e.tensor_tensor(o, a, m, ALU.mult),
                      (pvk, ("rb",)), (("ccT", cf),))
            for qt_i in range(4):
                for dc in range(2):
                    ps = C.PS[5]
                    pk = ("PS", 5)
                    for cf in range(8):
                        C.mm(ps, ccT[cf][:, 128 * qt_i:128 * (qt_i + 1)],
                             wreg[cf][:, 512 * dc:512 * (dc + 1)],
                             (("ccT", cf), (("WF",), cf)), pk,
                             start=(cf == 0), stop=(cf == 7))
                    C.activ(ostg[:, 512 * dc:512 * (dc + 1)], ps, AF.Copy,
                            (pk,), ("ostg",))
                ti = 16 * b + 4 * qb + qt_i
                C.st(drr[ti], ostg, (("ostg",),))


def stage_G(C):
    stage_f = [C.f32(2048) for _ in range(2)]
    stage_b = [C.bf(2048) for _ in range(2)]
    s12 = C.w("f1w12T").rearrange("e (c p) j -> (e c) p j", p=128)
    d12 = C.act("w12bf").rearrange("e (c p) j -> (e c) p j", p=128)
    for i in range(64):
        for half in range(2):
            r = half
            C.ld(stage_f[r], s12[i][:, 2048 * half:2048 * (half + 1)],
                 ("cvf", r))
            C.activ(stage_b[r], stage_f[r], AF.Copy, (("cvf", r),),
                    ("cvb", r))
            C.st(d12[i][:, 2048 * half:2048 * (half + 1)], stage_b[r],
                 (("cvb", r),), wkey=("w12bf",))
    s3 = C.w("f1w3T").rearrange("e (c p) d -> (e c) p d", p=128)
    d3 = C.act("w3bf").rearrange("e (c p) d -> (e c) p d", p=128)
    for i in range(128):
        r = i % 2
        C.ld(stage_f[r][:, 0:1024], s3[i], ("cvf", r))
        C.activ(stage_b[r][:, 0:1024], stage_f[r][:, 0:1024], AF.Copy,
                (("cvf", r),), ("cvb", r))
        C.st(d3[i], stage_b[r][:, 0:1024], (("cvb", r),), wkey=("w3bf",))
    gwr = C.w("f1gwT").rearrange("(c p) e -> c p e", p=128)
    gwB = C.bf(64)
    for c in range(8):
        sk = ("cvf", c % 2)
        C.ld(stage_f[c % 2][:, 0:8], gwr[c], sk)
        C.activ(gwB[:, 8 * c:8 * (c + 1)], stage_f[c % 2][:, 0:8], AF.Copy,
                (sk,), ("gwB", c))
    xnTs = [C.bf(512) for _ in range(8)]
    h2b = [C.bf(512) for _ in range(16)]
    hid = [C.bf(512) for _ in range(16)]
    w12c = [C.bf(1024) for _ in range(2)]
    w3c = [C.bf(1024) for _ in range(16)]
    yac = [C.f32(1024) for _ in range(4)]
    sgate = C.f32(48)
    probs = C.f32(32)
    silu_t = C.f32(512)
    fstg = [C.f32(1024) for _ in range(2)]
    xnTr = C.act("xn1T").rearrange("(c p) t -> c p t", p=128)
    x1r = C.act("x1tot").rearrange("(n p) d -> n p d", p=128)
    a3r = C.act("attn3").rearrange("(n p) d -> n p d", p=128)
    outr = C.act("out").rearrange("(n p) d -> n p d", p=128)
    w12br = C.act("w12bf").rearrange("e (c p) j -> e c p j", p=128)
    w3br = C.act("w3bf").rearrange("e (c p) d -> (e c) p d", p=128)
    for sl in range(NSLAB):
        for cd in range(8):
            C.ld(xnTs[cd], xnTr[cd, :, 512 * sl:512 * (sl + 1)],
                 ("xnTs", cd))
        for k in range(4):
            C.vec(lambda e, o=yac[k]: e.memset(o, 0.0), (), (("yac", k),))
            ps = C.PS[4]
            pk = ("PS", 4)
            for cd in range(8):
                C.mm(ps[:, 0:8], xnTs[cd][:, 128 * k:128 * (k + 1)],
                     gwB[:, 8 * cd:8 * (cd + 1)],
                     (("xnTs", cd), ("gwB", cd)), pk,
                     start=(cd == 0), stop=(cd == 7))
            sc8 = sgate[:, 12 * k:12 * k + 8]
            m1 = sgate[:, 12 * k + 8:12 * k + 9]
            m2 = sgate[:, 12 * k + 9:12 * k + 10]
            zz = sgate[:, 12 * k + 10:12 * k + 11]
            pr8 = probs[:, 8 * k:8 * k + 8]
            C.vec(lambda e, o=sc8, i_=ps[:, 0:8]: e.tensor_copy(o, i_),
                  (pk,), (("sc", k),))
            C.vec(lambda e, o=m1, i_=sc8: e.tensor_reduce(
                o, i_, AX.X, ALU.max), (("sc", k),), (("m1", k),))
            C.vec(lambda e, o=pr8, i_=sc8, s=m1:
                  e.tensor_scalar_sub(o, i_, s),
                  (("sc", k), ("m1", k)), (("d1", k),))
            C.activ(pr8, pr8, AF.Sign, (("d1", k),), ("d1", k))
            C.sadd(pr8, pr8, 1.0, (("d1", k),), ("d1", k))
            C.vec(lambda e, o=pr8, i0=pr8, s=-1.0e9, i1=sc8:
                  e.scalar_tensor_tensor(o, i0, s, i1, ALU.mult, ALU.add),
                  (("d1", k), ("sc", k)), (("msk", k),))
            C.vec(lambda e, o=m2, i_=pr8: e.tensor_reduce(
                o, i_, AX.X, ALU.max), (("msk", k),), (("m2", k),))
            C.vec(lambda e, o=pr8, i_=sc8, s=m2:
                  e.tensor_scalar_sub(o, i_, s),
                  (("sc", k), ("m2", k)), (("d2", k),))
            C.activ(pr8, pr8, AF.Sign, (("d2", k),), ("d2", k))
            C.activ(pr8, pr8, AF.Sign, (("d2", k),), ("d2", k), bias=1.0)
            C.vec(lambda e, o=sc8, i_=sc8, s=m1:
                  e.tensor_scalar_sub(o, i_, s),
                  (("sc", k), ("m1", k)), (("sc", k),))
            C.activ(sc8, sc8, AF.Exp, (("sc", k),), ("sc", k))
            C.vec(lambda e, o=pr8, a=pr8, b=sc8:
                  e.tensor_tensor(o, a, b, ALU.mult),
                  (("d2", k), ("sc", k)), (("u", k),))
            C.vec(lambda e, o=zz, i_=pr8: e.reduce_sum(o, i_, axis=AX.X),
                  (("u", k),), (("z", k),))
            C.vec(lambda e, o=zz, i_=zz: e.reciprocal(o, i_),
                  (("z", k),), (("z", k),))
            C.vec(lambda e, o=pr8, i_=pr8, s=zz:
                  e.tensor_scalar_mul(o, i_, s),
                  (("u", k), ("z", k)), (("pr", k),))
        for e_ in range(8):
            for jj in range(32):
                j = (16 + jj) if jj < 16 else (jj - 16)
                r = jj % 2
                C.ld(w12c[r].rearrange("p (c j) -> p c j", j=128),
                     w12br[e_, :, :, 128 * j:128 * (j + 1)].rearrange(
                         "c p j -> p c j"),
                     ("w12c", r), rkeys=(("w12bf",),))
                ps = C.PS[jj % 2]
                pk = ("PS", jj % 2)
                for c in range(8):
                    C.mm(ps, w12c[r][:, 128 * c:128 * (c + 1)], xnTs[c],
                         (("w12c", r), ("xnTs", c)), pk,
                         start=(c == 0), stop=(c == 7))
                if j >= 16:
                    C.activ(h2b[j - 16], ps, AF.Copy, (pk,), ("h2", j - 16))
                else:
                    C.activ(silu_t, ps, AF.Silu, (pk,), ("silu",))
                    C.vec(lambda e, o=hid[j], a=silu_t, b=h2b[j]:
                          e.tensor_tensor(o, a, b, ALU.mult),
                          (("silu",), ("h2", j)), (("hid", j),))
                    C.ld(w3c[j], w3br[16 * e_ + j], ("w3c", j),
                         rkeys=(("w3bf",),))
            for tt in range(4):
                for dc in range(2):
                    ps = C.PS[2 + dc]
                    pk = ("PS", 2 + dc)
                    for hc in range(16):
                        C.mm(ps, hid[hc][:, 128 * tt:128 * (tt + 1)],
                             w3c[hc][:, 512 * dc:512 * (dc + 1)],
                             (("hid", hc), ("w3c", hc)), pk,
                             start=(hc == 0), stop=(hc == 15))
                    C.vec(lambda e, o=yac[tt][:, 512 * dc:512 * (dc + 1)],
                          i0=ps, s=probs[:, 8 * tt + e_:8 * tt + e_ + 1],
                          i1=yac[tt][:, 512 * dc:512 * (dc + 1)]:
                          e.scalar_tensor_tensor(o, i0, s, i1, ALU.mult,
                                                 ALU.add),
                          (pk, ("pr", tt), ("yac", tt)), (("yac", tt),))
        for k in range(4):
            i = 4 * sl + k
            r = k % 2
            C.ld(fstg[r], x1r[i], ("fstg", r))
            C.vec(lambda e, o=yac[k], a=yac[k], b=fstg[r]:
                  e.tensor_tensor(o, a, b, ALU.add),
                  (("yac", k), ("fstg", r)), (("yac", k),))
            C.ld(fstg[r], a3r[i], ("fstg", r))
            C.vec(lambda e, o=yac[k], a=yac[k], b=fstg[r]:
                  e.tensor_tensor(o, a, b, ALU.add),
                  (("yac", k), ("fstg", r)), (("yac", k),))
            C.st(outr[i], yac[k], (("yac", k),))


STAGES = {"A": stage_A, "B": stage_B, "C": stage_C, "D": stage_D,
          "E": stage_E, "F": stage_F, "G": stage_G}
ORDER = "ABCDEFG"


def build_group(group):
    nc = bass.Bass("TRN2", target_bir_lowering=False)
    C = Ctx(nc, group)
    C.ld(C.cscal[:, :], C.w("cScal"), ("cscal",))
    C.P.barrier()
    for s in group:
        C.reset_arena()
        STAGES[s](C)
        C.P.barrier()
    C.P.replay(nc)
    C.close()
    return nc, C


def prep_host(inputs):
    f32 = np.float32
    h = {}
    h["l0Wq"] = np.ascontiguousarray(
        inputs["L0_attn_Wq"].transpose(1, 0, 2).reshape(D, D).astype(f32))
    h["l0Wk"] = np.ascontiguousarray(
        inputs["L0_attn_Wk"].transpose(1, 0, 2).reshape(D, D).astype(f32))
    h["l0Wv"] = np.ascontiguousarray(
        inputs["L0_attn_Wv"].transpose(1, 0, 2).reshape(D, D).astype(f32))
    h["l0WoT"] = np.ascontiguousarray(inputs["L0_attn_Wo"].T.astype(f32))
    for nm, src in (("f0Wq", "L0_ffn_Wq"), ("f0Wk", "L0_ffn_Wk"),
                    ("f0Wv", "L0_ffn_Wv"), ("f0Wo", "L0_ffn_Wo")):
        h[nm] = np.ascontiguousarray(np.asarray(inputs[src]).astype(f32))
    for nm, src in (("f0bq", "L0_ffn_bq"), ("f0bk", "L0_ffn_bk"),
                    ("f0bv", "L0_ffn_bv"), ("f0bo", "L0_ffn_bo"),
                    ("f0g1", "L0_ffn_ln1_g"), ("f0b1n", "L0_ffn_ln1_b"),
                    ("f0g2", "L0_ffn_ln2_g"), ("f0b2n", "L0_ffn_ln2_b")):
        h[nm] = np.ascontiguousarray(
            np.asarray(inputs[src]).reshape(1, D).astype(f32))
    h["f0rw"] = np.ascontiguousarray(
        np.asarray(inputs["L0_ffn_router_w"]).reshape(1, D).astype(f32))
    h["f0rb"] = np.ascontiguousarray(
        np.asarray(inputs["L0_ffn_router_b"]).reshape(1, 1).astype(f32))
    h["f0W1"] = np.ascontiguousarray(np.asarray(inputs["L0_ffn_W1"]).astype(f32))
    h["f0b1"] = np.ascontiguousarray(np.asarray(inputs["L0_ffn_b1"]).astype(f32))
    h["f0W2"] = np.ascontiguousarray(np.asarray(inputs["L0_ffn_W2"]).astype(f32))
    h["f0b2"] = np.ascontiguousarray(np.asarray(inputs["L0_ffn_b2"]).astype(f32))
    h["l1Wq"] = np.ascontiguousarray(np.asarray(inputs["L1_attn_Wq"]).astype(f32))
    h["l1Wkv"] = np.ascontiguousarray(
        np.asarray(inputs["L1_attn_Wkv"]).astype(f32))
    h["l1Wk"] = np.ascontiguousarray(np.asarray(inputs["L1_attn_Wk"]).astype(f32))
    h["l1Wv"] = np.ascontiguousarray(np.asarray(inputs["L1_attn_Wv"]).astype(f32))
    h["l1Wo"] = np.ascontiguousarray(np.asarray(inputs["L1_attn_Wo"]).astype(f32))
    h["l1qg"] = np.ascontiguousarray(
        np.tile(np.asarray(inputs["L1_attn_q_g"]).astype(f32), 2).reshape(128, 1))
    kgv = np.asarray(inputs["L1_attn_k_g"]).astype(f32)
    h["l1kg"] = np.ascontiguousarray(np.concatenate(
        [kgv, np.zeros(32, f32), kgv, np.zeros(32, f32)]).reshape(128, 1))
    h["f1w12T"] = np.ascontiguousarray(
        np.asarray(inputs["L1_ffn_w12"]).transpose(0, 2, 1).astype(f32))
    h["f1w3T"] = np.ascontiguousarray(
        np.asarray(inputs["L1_ffn_w3"]).transpose(0, 2, 1).astype(f32))
    h["f1gwT"] = np.ascontiguousarray(
        np.asarray(inputs["L1_ffn_gate_w"]).T.astype(f32))
    h["cIdent"] = np.eye(128, dtype=BF)
    msk = np.zeros((4, 128, 512), np.float32)
    p = np.arange(128)[:, None]
    c = np.arange(512)[None, :]
    for jl in range(4):
        msk[jl] = np.where(p <= c - 128 * jl, 0.0, NEGBIG)
    h["cMask"] = msk
    blk = np.zeros((128, 2), BF)
    blk[0:64, 0] = 1
    blk[64:128, 1] = 1
    h["cBlk64"] = blk
    h["cBlk64T"] = np.ascontiguousarray(blk.T)
    cs = np.zeros((1, 32), np.float32)
    cs[0, :len(CONST_VALS)] = np.array(CONST_VALS, np.float32)
    h["cScal"] = np.ascontiguousarray(np.tile(cs, (128, 1)))
    return h


def kernel(**inputs):
    host = prep_host(inputs)
    xs = np.ascontiguousarray(np.asarray(inputs["L0_attn_x"]).astype(np.float32))
    shards = [np.ascontiguousarray(xs[2 * i:2 * i + 2]) for i in range(NCORE)]
    scratch = {}
    groups = [ORDER] if MODE == "fused" else [s for s in ORDER]
    for g in groups:
        nc, C = build_group(g)
        in_maps = []
        for c in range(NCORE):
            m = {}
            for n in C.used_in:
                if n == "x":
                    m[n] = shards[c]
                elif n in host:
                    m[n] = host[n]
                else:
                    m[n] = scratch[n][c]
            in_maps.append(m)
        res = run_bass_kernel_spmd(nc, in_maps, core_ids=list(range(NCORE)))
        for n in C.outs:
            scratch[n] = [np.asarray(res.results[c][n]) for c in range(NCORE)]
    outs = scratch["out"]
    return np.concatenate(
        [o.reshape(B_LOC, S, D) for o in outs], axis=0).astype(np.float32)
